# version 01
# baseline (speedup 1.0000x reference)
import numpy as np
import ml_dtypes
import concourse.bass as bass
import concourse.mybir as mybir
from concourse.alu_op_type import AluOpType as Op
from concourse.bass_utils import run_bass_kernel_spmd

F32 = mybir.dt.float32
BF16 = mybir.dt.bfloat16
AF = mybir.ActivationFunctionType
AX = mybir.AxisListType

B, T, D = 2, 16384, 2048
NH, DK, DV, CHK = 16, 128, 128, 64
TOWN, HALO = 4096, 512
TEXT = TOWN + HALO
NTL = TEXT // 512
NOTL = TOWN // 512
NCHK = TEXT // CHK
KC = D // 128
NPAIR = NH // 2
E, IMOE, ISH = 8, 1408, 5632

ENGINES = ("tensor", "vector", "scalar", "gpsimd", "sync")
ALL_PHASES = ("p0", "ln", "p1b", "p1q", "p1k", "p1v", "scan", "epi", "ffn")

SCRATCH_FLOW = {
    "gup_b": ("p0", ["ffn"]), "dwn_b": ("p0", ["ffn"]), "shg_b": ("p0", ["ffn"]),
    "shu_b": ("p0", ["ffn"]), "shd_b": ("p0", ["ffn"]),
    "xnf": ("ln", ["p1b", "p1q", "p1k", "p1v", "ffn"]),
    "eb_s": ("p1b", ["p1q", "p1k"]), "enb_s": ("p1b", ["p1k"]),
    "ebc_s": ("p1b", ["scan"]), "betaf_s": ("p1b", ["p1k", "p1v"]),
    "gate_s": ("p1b", ["epi"]),
    "qt_s": ("p1q", ["scan"]), "bkt_s": ("p1k", ["scan"]),
    "kb_s": ("p1k", ["scan"]), "vb_s": ("p1v", ["scan"]),
    "o_s": ("scan", ["epi"]), "og_s": ("epi", ["epi"]),
}


class Ctx:
    def __init__(self, nc, stack):
        self.nc = nc
        self.prog = {e: [] for e in ENGINES}
        self.sems = {}
        self.cnt = {}
        self.phase_idx = 0
        self.enforced = {e: {} for e in ENGINES}
        for s in ["bar", "Ls", "Lg", "St", "Sc", "T", "V", "A", "P", "G",
                  "XA", "XB", "EA", "EB", "BL", "SA", "SB", "TA", "TB",
                  "OA", "XL", "TL", "WL", "WQ"]:
            self.sems[s] = stack.enter_context(nc.semaphore(s))
            self.cnt[s] = 0

    def emit(self, eng, fn):
        self.prog[eng].append(fn)

    def inc(self, name, amt):
        self.cnt[name] += amt
        return (name, self.cnt[name])

    def all_of(self, sem):
        return (sem, self.cnt[sem])

    def wait(self, eng_name, deps):
        if deps is None:
            return
        if isinstance(deps, tuple):
            deps = [deps]
        best = {}
        for d in deps:
            if d is None:
                continue
            sn, v = d
            if v > best.get(sn, -1):
                best[sn] = v
        todo = []
        for sn, v in best.items():
            if self.enforced[eng_name].get(sn, -1) >= v:
                continue
            self.enforced[eng_name][sn] = v
            todo.append((self.sems[sn], v))
        if todo:
            def fn(eng, todo=todo):
                for s, v in todo:
                    eng.wait_ge(s, v)
            self.emit(eng_name, fn)

    def op(self, eng_name, fn, deps=None, inc=None):
        self.wait(eng_name, deps)
        if inc is None:
            self.emit(eng_name, fn)
            return None
        mark = self.inc(inc, 1)
        s = self.sems[inc]

        def fn2(eng, fn=fn, s=s):
            fn(eng).then_inc(s, 1)
        self.emit(eng_name, fn2)
        return mark

    def dma(self, queue, outap, inap, deps=None, sem=None, transpose=False,
            accum=False):
        if sem is None:
            sem = {"sync": "Ls", "scalar": "Sc", "gpsimd": "Lg"}[queue]
        self.wait(queue, deps)
        mark = self.inc(sem, 16)
        s = self.sems[sem]

        def fn(eng, outap=outap, inap=inap, s=s, transpose=transpose, accum=accum):
            kw = {}
            if transpose:
                kw["transpose"] = True
            if accum:
                kw["accum_op"] = Op.add
            eng.dma_start(out=outap, in_=inap, **kw).then_inc(s, 16)
        self.emit(queue, fn)
        return mark

    def barrier(self):
        self.phase_idx += 1
        tgt = 5 * self.phase_idx
        sem = self.sems["bar"]
        self.cnt["bar"] = tgt
        dma_sems = [(self.sems[sn], self.cnt[sn]) for sn in self.sems
                    if sn not in ("bar", "T", "V", "A", "P") and self.cnt[sn] > 0]
        own = {"tensor": "T", "vector": "V", "scalar": "A", "gpsimd": "P"}
        for e in ENGINES:
            waits = list(dma_sems)
            if e in own and self.cnt[own[e]] > 0:
                waits.append((self.sems[own[e]], self.cnt[own[e]]))
            def fn(eng, sem=sem, tgt=tgt, waits=waits):
                for s, v in waits:
                    eng.wait_ge(s, v)
                eng.sem_inc(sem, 1)
                eng.wait_ge(sem, tgt)
            self.emit(e, fn)
            for sn in self.sems:
                self.enforced[e][sn] = self.cnt[sn]


class Carver:
    def __init__(self, tensor, limit):
        self.t, self.limit, self.off = tensor, limit, 0

    def take(self, *shape):
        n = int(np.prod(shape))
        assert self.off + n <= self.limit, f"arena overflow {self.off}+{n}>{self.limit}"
        ap = self.t[:, self.off:self.off + n]
        self.off += n
        if len(shape) > 1:
            names = " ".join(f"d{i}" for i in range(len(shape)))
            ap = ap.rearrange(f"p ({names}) -> p {names}",
                              **{f"d{i}": s for i, s in enumerate(shape)})
        return ap

    def reset(self):
        self.off = 0


def fm_tile(dram2d, t0, tn):
    return dram2d[:, t0:t0 + tn].rearrange("(c p) t -> p c t", p=128)


def build_program(phases=ALL_PHASES, dbg=()):
    import contextlib
    nc = bass.Bass("TRN2", target_bir_lowering=False)
    stack = contextlib.ExitStack()

    W = {}
    for n, shp in [("Wq", [D, D]), ("Wk", [D, D]), ("Wv", [D, D]), ("Wo", [D, D]),
                   ("Wad", [D, DK]), ("Wau", [DK, D]), ("Wb", [D, NH]),
                   ("Wgd", [D, DV]), ("Wgu", [DV, D]),
                   ("cqw", [D, 4]), ("ckw", [D, 4]), ("cvw", [D, 4]),
                   ("rmsw", [DV]), ("rw", [E, D]), ("gup", [E, 2 * IMOE, D]),
                   ("dwn", [E, D, IMOE]), ("shg", [ISH, D]), ("shu", [ISH, D]),
                   ("shd", [D, ISH]), ("sgw", [1, D])]:
        W[n] = nc.dram_tensor(n, shp, F32, kind="ExternalInput").ap()
    x_ext = nc.dram_tensor("x_ext", [TEXT + 3, D], F32, kind="ExternalInput").ap()
    identb = nc.dram_tensor("identb", [128, 128], BF16, kind="ExternalInput").ap()
    maskN = nc.dram_tensor("maskN", [128, 512], F32, kind="ExternalInput").ap()
    maskU = nc.dram_tensor("maskU", [128, 512], F32, kind="ExternalInput").ap()
    out = nc.dram_tensor("out", [TOWN, D], F32, kind="ExternalOutput").ap()

    S = {}
    shapes = {
        "gup_b": ([E, 2 * IMOE, D], BF16), "dwn_b": ([E, D, IMOE], BF16),
        "shg_b": ([ISH, D], BF16), "shu_b": ([ISH, D], BF16),
        "shd_b": ([D, ISH], BF16),
        "xnf": ([D, TEXT], BF16), "eb_s": ([D, TEXT], BF16),
        "enb_s": ([D, TEXT], BF16), "ebc_s": ([D, NCHK], F32),
        "betaf_s": ([NH, TEXT], F32), "gate_s": ([TEXT, D], BF16),
        "qt_s": ([D, TEXT], BF16), "bkt_s": ([D, TEXT], BF16),
        "kb_s": ([D, TEXT], BF16), "vb_s": ([D, TEXT], BF16),
        "o_s": ([TEXT, D], BF16), "og_s": ([TOWN, D], BF16),
    }
    for name, (shp, dt) in shapes.items():
        prod, cons = SCRATCH_FLOW[name]
        has_p = prod in phases
        has_c = any(c in phases for c in cons)
        if name in dbg or (has_p and not has_c):
            kind = "ExternalOutput"
        elif has_c and not has_p:
            kind = "ExternalInput"
        elif has_p or has_c:
            kind = "Internal"
        else:
            continue
        S[name] = nc.dram_tensor(name, shp, dt, kind=kind).ap()

    AB = stack.enter_context(nc.sbuf_tensor("AB", [128, 66750], BF16))
    AFT = stack.enter_context(nc.sbuf_tensor("AFt", [128, 9000], F32))
    ident_sb = stack.enter_context(nc.sbuf_tensor("ident_sb", [128, 128], BF16))
    maskN_sb = stack.enter_context(nc.sbuf_tensor("maskN_sb", [128, 512], F32))
    maskU_sb = stack.enter_context(nc.sbuf_tensor("maskU_sb", [128, 512], F32))
    cw_sb = stack.enter_context(nc.sbuf_tensor("cw_sb", [128, 3, 16, 4], F32))
    comb_sb = stack.enter_context(nc.sbuf_tensor("comb_sb", [128, 8, 4, 8], F32))
    sg2_sb = stack.enter_context(nc.sbuf_tensor("sg2_sb", [128, 8, 4], F32))
    rmsbc_sb = stack.enter_context(nc.sbuf_tensor("rmsbc_sb", [128, 128], F32))
    onesb_sb = stack.enter_context(nc.sbuf_tensor("onesb_sb", [128, 1], BF16))
    ones1_sb = stack.enter_context(nc.sbuf_tensor("ones1_sb", [1, 128], F32))
    brow_sb = stack.enter_context(nc.sbuf_tensor("brow_sb", [1, NH, 512], F32))

    PF = [stack.enter_context(nc.psum_tensor(f"pf{i}", [128, 512], F32))
          for i in range(6)]
    PBF = [stack.enter_context(nc.psum_tensor(f"pbb{i}", [128, 1024], BF16))
           for i in range(2)]

    ctx = Ctx(nc, stack)
    fb = Carver(AFT, 9000)
    bb = Carver(AB, 66750)
    ns = dict(W=W, S=S, x_ext=x_ext, out=out, ident_sb=ident_sb,
              maskN_sb=maskN_sb, maskU_sb=maskU_sb, cw_sb=cw_sb,
              comb_sb=comb_sb, sg2_sb=sg2_sb, rmsbc_sb=rmsbc_sb,
              onesb_sb=onesb_sb, ones1_sb=ones1_sb, brow_sb=brow_sb,
              PF=PF, PBF=PBF, fb=fb, bb=bb)

    ctx.dma("sync", ident_sb[:, :], identb)
    ctx.dma("sync", maskN_sb[:, :], maskN)
    ctx.dma("sync", maskU_sb[:, :], maskU)
    ctx.dma("sync", cw_sb[:, 0], W["cqw"].rearrange("(c p) k -> p c k", p=128))
    ctx.dma("sync", cw_sb[:, 1], W["ckw"].rearrange("(c p) k -> p c k", p=128))
    cm = ctx.dma("sync", cw_sb[:, 2], W["cvw"].rearrange("(c p) k -> p c k", p=128))
    eps_sb = stack.enter_context(nc.sbuf_tensor("eps_sb", [128, 1], F32))
    nc.const_aps.register(F32, 1e-6, eps_sb[:, :])
    ctx.op("vector", lambda e: e.memset(eps_sb[:, :], 1e-6))
    ctx.op("vector", lambda e: e.memset(onesb_sb[:, :], 1.0))
    m1 = ctx.op("vector", lambda e: e.memset(ones1_sb[:, :], 1.0), inc="V")
    ctx.wait("tensor", [cm, m1])
    ctx.barrier()

    emitters = {"p0": emit_p0, "ln": emit_ln, "p1b": emit_p1b,
                "p1q": lambda c, n: _qkv_pass(c, n, "q"),
                "p1k": lambda c, n: _qkv_pass(c, n, "k"),
                "p1v": lambda c, n: _qkv_pass(c, n, "v"),
                "scan": emit_scan, "epi": emit_epi, "ffn": emit_ffn}
    for ph in ALL_PHASES:
        if ph not in phases:
            continue
        fb.reset(); bb.reset()
        emitters[ph](ctx, ns)
        ctx.barrier()

    for q in ("sync", "scalar", "gpsimd"):
        ctx.wait(q, [("St", ctx.cnt["St"]), ("Sc", ctx.cnt["Sc"]),
                     ("G", ctx.cnt["G"]), ("Lg", ctx.cnt["Lg"])])

    with nc.Block() as block:
        @block.tensor
        def _(eng):
            for fn in ctx.prog["tensor"]:
                fn(eng)

        @block.vector
        def _(eng):
            for fn in ctx.prog["vector"]:
                fn(eng)

        @block.scalar
        def _(eng):
            for fn in ctx.prog["scalar"]:
                fn(eng)

        @block.gpsimd
        def _(eng):
            for fn in ctx.prog["gpsimd"]:
                fn(eng)

        @block.sync
        def _(eng):
            for fn in ctx.prog["sync"]:
                fn(eng)

    return nc, stack


def emit_p0(ctx, ns):
    W, S = ns["W"], ns["S"]
    for e_i in range(E):
        ctx.dma("gpsimd", S["gup_b"][e_i], W["gup"][e_i])
        ctx.dma("gpsimd", S["dwn_b"][e_i], W["dwn"][e_i])
    ctx.dma("gpsimd", S["shg_b"], W["shg"])
    ctx.dma("gpsimd", S["shu_b"], W["shu"])
    ctx.dma("gpsimd", S["shd_b"], W["shd"])


def emit_ln(ctx, ns):
    S, x_ext = ns["S"], ns["x_ext"]
    fb, bb, PBF = ns["fb"], ns["bb"], ns["PBF"]
    ident_sb = ns["ident_sb"]
    xin = fb.take(1, 4, 2048)
    sumc = fb.take(4, 4)
    ssq = fb.take(4, 1)
    rstd = fb.take(4, 1)
    xnt = bb.take(4, 2048)
    xnfst = bb.take(16, 512)

    loads = [ctx.dma("sync", xin[:, 0],
                     x_ext[3:3 + 512, :].rearrange("(g p) d -> p g d", p=128))]
    bank_guard = [None, None]
    xnt_guard = {g: [None, None] for g in range(4)}
    for ti in range(NTL):
        for g in range(4):
            xg = xin[:, 0, g]
            dep = loads[ti] if g == 0 else None
            m0 = ctx.op("vector", lambda e, xg=xg, g=g:
                        e.reduce_sum(sumc[:, g, 0:1], xg, axis=AX.X),
                        deps=dep, inc="V")
            ma = ctx.op("scalar", lambda e, xg=xg, g=g:
                        e.activation(xnt[:, g], xg, AF.Square,
                                     accum_out=ssq[:, g]),
                        deps=([dep] if dep else []) + list(xnt_guard[g]), inc="A")
            m1 = ctx.op("vector", lambda e, g=g:
                        e.tensor_single_scalar(sumc[:, g, 1:2], sumc[:, g, 0:1],
                                               1.0 / 2048, Op.mult),
                        deps=m0, inc="V")
            m2 = ctx.op("vector", lambda e, g=g:
                        e.tensor_tensor(sumc[:, g, 2:3], sumc[:, g, 1:2],
                                        sumc[:, g, 1:2], Op.mult),
                        deps=m1, inc="V")
            m3 = ctx.op("vector", lambda e, g=g:
                        e.tensor_scalar(sumc[:, g, 3:4], ssq[:, g],
                                        1.0 / 2048, sumc[:, g, 2:3],
                                        Op.mult, Op.subtract),
                        deps=[m2, ma], inc="V")
            m4 = ctx.op("vector", lambda e, g=g:
                        e.tensor_single_scalar(rstd[:, g], sumc[:, g, 3:4],
                                               1e-5, Op.add),
                        deps=m3, inc="V")
            mk2 = ctx.op("scalar", lambda e, g=g:
                         e.activation(rstd[:, g], rstd[:, g], AF.Sqrt),
                         deps=m4, inc="A")
            rq = ctx.op("vector", lambda e, g=g:
                        e.reciprocal(rstd[:, g], rstd[:, g]), deps=mk2, inc="V")
            mk3 = ctx.op("vector", lambda e, xg=xg, g=g:
                         e.tensor_scalar(xnt[:, g], xg, sumc[:, g, 1:2],
                                         rstd[:, g], Op.subtract, Op.mult),
                         deps=list(xnt_guard[g]) + [rq], inc="V")
            for half in range(2):
                pb = PBF[half]
                deps = [mk3, bank_guard[half]]
                tm = None
                for j in range(8):
                    kc = half * 8 + j
                    tm = ctx.op("tensor", lambda e, g=g, kc=kc, j=j, pb=pb:
                                e.transpose(pb[:, j * 128:(j + 1) * 128],
                                            xnt[:, g, kc * 128:(kc + 1) * 128],
                                            ident_sb[:, :]),
                                deps=deps if j == 0 else None,
                                inc="T" if j == 7 else None)
                bank_guard[half] = xnt_guard[g][half] = ctx.op(
                    "scalar", lambda e, g=g, half=half, pb=pb:
                    e.copy(xnfst[:, half * 8:half * 8 + 8,
                                 g * 128:(g + 1) * 128], pb[:, :]),
                    deps=[tm, ctx.all_of("St")] if (g == 0 and half == 0)
                    else tm, inc="A")
        ctx.dma("scalar", fm_tile(S["xnf"], ti * 512, 512), xnfst[:, :, :],
                deps=bank_guard[1], sem="St")
        if ti + 1 < NTL:
            loads.append(ctx.dma("sync", xin[:, 0],
                                 x_ext[3 + (ti + 1) * 512:3 + (ti + 2) * 512, :]
                                 .rearrange("(g p) d -> p g d", p=128),
                                 deps=[("V", ctx.cnt["V"])]))
        else:
            loads.append(None)


def xn_loader(ctx, ns, bufap, nbuf=2):
    S = ns["S"]

    def load(ti, deps=None):
        return ctx.dma("sync", bufap[:, ti % nbuf],
                       fm_tile(S["xnf"], ti * 512, 512), deps=deps,
                       sem="XA" if ti % 2 == 0 else "XB")
    return load


def emit_p1b(ctx, ns):
    W, S = ns["W"], ns["S"]
    fb, bb, PF, PBF = ns["fb"], ns["bb"], ns["PF"], ns["PBF"]
    ident_sb = ns["ident_sb"]

    xnfm = bb.take(2, 16, 512)
    wad_sb = bb.take(16, 128)
    wau_sb = bb.take(2048)
    wgd_sb = bb.take(16, 128)
    wgu_sb = bb.take(2048)
    wb_sb = bb.take(16, 16)
    g1s = bb.take(512)
    gt1s = bb.take(512)
    gateb = bb.take(2048)
    ebst = bb.take(512)
    betat = bb.take(64)
    gwk = fb.take(3, 512)
    cumf = fb.take(512)
    ebcf = fb.take(8)
    betar = fb.take(128)

    wl = [ctx.dma("gpsimd", wad_sb, W["Wad"].rearrange("(c p) n -> p c n", p=128), sem="WQ"),
          ctx.dma("gpsimd", wau_sb, W["Wau"], sem="WQ"),
          ctx.dma("gpsimd", wgd_sb, W["Wgd"].rearrange("(c p) n -> p c n", p=128), sem="WQ"),
          ctx.dma("gpsimd", wgu_sb, W["Wgu"], sem="WQ"),
          ctx.dma("gpsimd", wb_sb, W["Wb"].rearrange("(c p) n -> p c n", p=128), sem="WQ")]
    load = xn_loader(ctx, ns, xnfm)
    xw = [load(0), load(1)]
    guards = {i: None for i in range(8)}
    a3_guard = ac_guard = ebst_guard = ebcf_guard = betar_guard = None
    vc_prev = None

    for ti in range(NTL):
        p = ti % 2
        t0 = ti * 512
        deps = [xw[ti], guards[0]] + (wl if ti == 0 else [])
        g1m = None
        for kc in range(KC):
            g1m = ctx.op("tensor", lambda e, kc=kc, p=p:
                         e.matmul(PF[0][:, :], wad_sb[:, kc], xnfm[:, p, kc],
                                  start=(kc == 0), stop=(kc == 15)),
                         deps=deps if kc == 0 else None,
                         inc="T" if kc == 15 else None)
        guards[0] = ctx.op("vector", lambda e:
                           e.tensor_copy(g1s, PF[0][:, :]), deps=g1m, inc="V")
        g2m = None
        for kc in range(KC):
            g2m = ctx.op("tensor", lambda e, kc=kc, p=p:
                         e.matmul(PF[1][:, :], wgd_sb[:, kc], xnfm[:, p, kc],
                                  start=(kc == 0), stop=(kc == 15)),
                         deps=[guards[1]] if kc == 0 else None,
                         inc="T" if kc == 15 else None)
        guards[1] = ctx.op("vector", lambda e:
                           e.tensor_copy(gt1s, PF[1][:, :]), deps=g2m, inc="V")

        bm = None
        for g in range(4):
            for kc in range(KC):
                bm = ctx.op("tensor", lambda e, g=g, kc=kc, p=p:
                            e.matmul(PF[2][:, g * 16:(g + 1) * 16],
                                     xnfm[:, p, kc, g * 128:(g + 1) * 128],
                                     wb_sb[:, kc],
                                     start=(kc == 0), stop=(kc == 15)),
                            deps=[guards[2]] if (g == 0 and kc == 0) else None,
                            inc="T" if (g == 3 and kc == 15) else None)
        bclip = ctx.op("vector", lambda e:
                       e.tensor_scalar(betat, PF[2][:, 0:64], -10.0, 10.0,
                                       Op.max, Op.min), deps=bm, inc="V")
        guards[2] = bclip
        bsig = ctx.op("scalar", lambda e:
                      e.activation(betat, betat, AF.Sigmoid), deps=bclip, inc="A")
        bt = ctx.op("tensor", lambda e:
                    e.transpose(PBF[1][0:64, 0:128], betat, ident_sb[:, :]),
                    deps=[bsig, guards[7]], inc="T")
        guards[7] = ctx.op("vector", lambda e:
                           e.tensor_copy(betar[0:64, :], PBF[1][0:64, 0:128]),
                           deps=[bt, ctx.all_of("St")], inc="V")
        for g in range(4):
            betar_guard = ctx.dma(
                "scalar", S["betaf_s"][:, t0 + g * 128:t0 + (g + 1) * 128],
                betar[g * 16:(g + 1) * 16, :],
                deps=guards[7] if g == 0 else None, sem="St")

        for cc in range(KC):
            bk = 4 + (cc % 2)
            bank = PF[bk]
            g2 = ctx.op("tensor", lambda e, cc=cc, bank=bank:
                        e.matmul(bank[:, :], wau_sb[:, cc * 128:(cc + 1) * 128],
                                 g1s, start=True, stop=True),
                        deps=[guards[0], guards[bk]], inc="T")
            guards[bk] = vb = ctx.op("vector", lambda e, bank=bank:
                                     e.tensor_scalar(gwk[:, 0], bank[:, :],
                                                     -10.0, 10.0, Op.max, Op.min),
                                     deps=[g2, a3_guard, vc_prev], inc="V")
            ab1 = ctx.op("scalar", lambda e:
                         e.activation(gwk[:, 1], gwk[:, 0], AF.Abs), deps=vb,
                         inc="A")
            ab2 = ctx.op("scalar", lambda e:
                         e.activation(gwk[:, 1], gwk[:, 1], AF.Exp, scale=-1.0),
                         deps=ab1, inc="A")
            a2 = ctx.op("scalar", lambda e:
                        e.activation(gwk[:, 1], gwk[:, 1], AF.Ln, bias=1.0),
                        deps=ab2, inc="A")
            a3 = ctx.op("scalar", lambda e:
                        e.activation(gwk[:, 2], gwk[:, 0], AF.Relu), inc="A")
            a3_guard = a3
            vg = ctx.op("vector", lambda e:
                        e.scalar_tensor_tensor(gwk[:, 0], gwk[:, 2], -1.0,
                                               gwk[:, 1], Op.mult, Op.subtract),
                        deps=[a2, a3, vc_prev], inc="V")
            vc = None
            for sg8 in range(8):
                vc = ctx.op("vector", lambda e, sg8=sg8:
                            e.tensor_tensor_scan(cumf[:, sg8 * 64:(sg8 + 1) * 64],
                                                 gwk[:, 0, sg8 * 64:(sg8 + 1) * 64],
                                                 gwk[:, 0, sg8 * 64:(sg8 + 1) * 64],
                                                 0.0, Op.add, Op.bypass),
                            deps=[vg, ac_guard] if sg8 == 0 else None,
                            inc="V" if sg8 == 7 else None)
            vc_prev = vc
            ae = ctx.op("scalar", lambda e:
                        e.activation(ebst, cumf, AF.Exp),
                        deps=[vc, ctx.all_of("St")], inc="A")
            d1 = ctx.dma("sync", S["eb_s"][cc * 128:(cc + 1) * 128, t0:t0 + 512],
                         ebst, deps=ae, sem="St")
            ae2 = ctx.op("scalar", lambda e:
                         e.activation(ebst, cumf, AF.Exp, scale=-1.0),
                         deps=d1, inc="A")
            d2 = ctx.dma("sync", S["enb_s"][cc * 128:(cc + 1) * 128, t0:t0 + 512],
                         ebst, deps=ae2, sem="St")
            ebst_guard = None
            ac = ctx.op("scalar", lambda e:
                        e.activation(ebcf,
                                     cumf.rearrange("p (a b) -> p a b", b=64)[:, :, 63],
                                     AF.Exp), deps=ctx.all_of("St"), inc="A")
            ac_guard = ac
            ebcf_guard = ctx.dma("sync",
                                 S["ebc_s"][cc * 128:(cc + 1) * 128,
                                            ti * 8:ti * 8 + 8],
                                 ebcf, deps=ac, sem="St")
            guards[bk] = ctx.all_of("St")

        for g in range(4):
            glast = None
            for hc in range(4):
                bk = 4 + (hc % 2)
                bank = PF[bk]
                gm = ctx.op("tensor", lambda e, g=g, hc=hc, bank=bank:
                            e.matmul(bank[:, :], gt1s[:, g * 128:(g + 1) * 128],
                                     wgu_sb[:, hc * 512:(hc + 1) * 512],
                                     start=True, stop=True),
                            deps=[guards[1], guards[bk]], inc="T")
                guards[bk] = glast = ctx.op(
                    "scalar", lambda e, hc=hc, bank=bank:
                    e.activation(gateb[:, hc * 512:(hc + 1) * 512], bank[:, :],
                                 AF.Sigmoid), deps=gm, inc="A")
            gd = ctx.dma("sync", S["gate_s"][t0 + g * 128:t0 + (g + 1) * 128, :],
                         gateb, deps=glast, sem="St")
            guards[4] = guards[5] = ctx.all_of("St")
        if ti + 2 < NTL:
            xw.append(load(ti + 2, deps=[("T", ctx.cnt["T"])]))
        else:
            xw.append(None)


def _qkv_pass(ctx, ns, which):
    W, S = ns["W"], ns["S"]
    fb, bb, PF, PBF = ns["fb"], ns["bb"], ns["PF"], ns["PBF"]
    cw_sb, onesb_sb, ones1_sb, brow_sb = (ns["cw_sb"], ns["onesb_sb"],
                                          ns["ones1_sb"], ns["brow_sb"])
    ident_sb, x_ext = ns["ident_sb"], ns["x_ext"]

    wsb = bb.take(16, 2048)
    xnfm = bb.take(2, 16, 512)
    ebt = bb.take(2, 512)
    ebt2 = bb.take(2, 512)
    sqb = bb.take(512)
    stg = bb.take(512)
    stg2 = bb.take(512)
    xn3b = bb.take(2048)
    xn3f = bb.take(16, 3)
    qs = fb.take(515)
    conva = fb.take(512)
    silf = fb.take(512)
    knf = fb.take(512)
    rrow = fb.take(2, 512)
    bndry = fb.take(16, 3)
    x3 = fb.take(2048)
    st3 = fb.take(4)
    mv3 = fb.take(4)
    pre_f = fb.take(16, 3)

    wn = {"q": "Wq", "k": "Wk", "v": "Wv"}[which]
    cwi = {"q": 0, "k": 1, "v": 2}[which]
    wld = ctx.dma("gpsimd", wsb, W[wn].rearrange("(c p) n -> p c n", p=128), sem="WQ")
    load = xn_loader(ctx, ns, xnfm)
    xw = [load(0), load(1)]

    l3 = ctx.dma("sync", x3[0:3, :], x_ext[0:3, :])
    p0m = ctx.op("vector", lambda e:
                 e.reduce_sum(st3[0:3, 0:1], x3[0:3, :], axis=AX.X),
                 deps=l3, inc="V")
    p0a = ctx.op("scalar", lambda e:
                 e.activation(xn3b[0:3, :], x3[0:3, :], AF.Square,
                              accum_out=st3[0:3, 2:3]), deps=l3, inc="A")
    p1m = ctx.op("vector", lambda e:
                 e.tensor_single_scalar(mv3[0:3, 0:1], st3[0:3, 0:1],
                                        1.0 / 2048, Op.mult), deps=p0m, inc="V")
    p2m = ctx.op("vector", lambda e:
                 e.tensor_tensor(mv3[0:3, 2:3], mv3[0:3, 0:1], mv3[0:3, 0:1],
                                 Op.mult), deps=p1m, inc="V")
    p3m = ctx.op("vector", lambda e:
                 e.tensor_scalar(mv3[0:3, 1:2], st3[0:3, 2:3], 1.0 / 2048,
                                 mv3[0:3, 2:3], Op.mult, Op.subtract),
                 deps=[p2m, p0a], inc="V")
    p4m = ctx.op("vector", lambda e:
                 e.tensor_single_scalar(mv3[0:3, 1:2], mv3[0:3, 1:2],
                                        1e-5, Op.add), deps=p3m, inc="V")
    a3m = ctx.op("scalar", lambda e:
                 e.activation(mv3[0:3, 1:2], mv3[0:3, 1:2], AF.Sqrt),
                 deps=p4m, inc="A")
    p5m = ctx.op("vector", lambda e:
                 e.reciprocal(mv3[0:3, 1:2], mv3[0:3, 1:2]),
                 deps=a3m, inc="V")
    xn3m = ctx.op("vector", lambda e:
                  e.tensor_scalar(xn3b[0:3, :], x3[0:3, :], mv3[0:3, 0:1],
                                  mv3[0:3, 1:2], Op.subtract, Op.mult),
                  deps=p5m, inc="V")
    t3 = None
    for kc in range(KC):
        t3 = ctx.op("tensor", lambda e, kc=kc:
                    e.transpose(PBF[0][:, kc * 4:kc * 4 + 3],
                                xn3b[0:3, kc * 128:(kc + 1) * 128],
                                ident_sb[0:3, 0:3]),
                    deps=xn3m if kc == 0 else None,
                    inc="T" if kc == 15 else None)
    ev3 = ctx.op("scalar", lambda e:
                 e.copy(xn3f,
                        PBF[0][:, 0:64].rearrange("p (a b) -> p a b", b=4)[:, :, 0:3]),
                 deps=t3, inc="A")
    pm = None
    for cc in range(KC):
        for kc in range(KC):
            pm = ctx.op("tensor", lambda e, cc=cc, kc=kc:
                        e.matmul(PF[3][:, cc * 3:(cc + 1) * 3],
                                 wsb[:, kc, cc * 128:(cc + 1) * 128],
                                 xn3f[:, kc],
                                 start=(kc == 0), stop=(kc == 15)),
                        deps=[ev3, wld] if (cc == 0 and kc == 0) else None,
                        inc="T" if (cc == 15 and kc == 15) else None)
    pv = ctx.op("vector", lambda e:
                e.tensor_copy(pre_f.rearrange("p a b -> p (a b)"), PF[3][:, 0:48]),
                deps=pm, inc="V")
    binit = ctx.op("vector", lambda e:
                   e.tensor_copy(bndry.rearrange("p a b -> p (a b)"),
                                 pre_f.rearrange("p a b -> p (a b)")),
                   deps=pv, inc="V")

    guards = {0: None, 1: None}
    bc_guard = pv
    sm_guard = None
    qs_guard = silf_guard = rr_guard = None
    stg_guard = stg2_guard = None
    prev_bnd = {cc: binit for cc in range(KC)}
    el_guard = [None, None]
    needs_beta = which in ("k", "v")
    bload = []
    if needs_beta:
        bload.append(ctx.dma("sync", brow_sb[:, :, :],
                             S["betaf_s"][:, 0:512].rearrange("(o h) t -> o h t", o=1),
                             sem="BL"))

    for ti in range(NTL):
        p = ti % 2
        t0 = ti * 512
        for cc in range(KC):
            gk = cc % 2
            bank = PF[gk]
            deps = [xw[ti], guards[gk]] + ([wld] if (ti == 0 and cc == 0) else [])
            mm = None
            for kc in range(KC):
                mm = ctx.op("tensor", lambda e, cc=cc, kc=kc, p=p, bank=bank:
                            e.matmul(bank[:, :],
                                     wsb[:, kc, cc * 128:(cc + 1) * 128],
                                     xnfm[:, p, kc],
                                     start=(kc == 0), stop=(kc == 15)),
                            deps=deps if kc == 0 else None,
                            inc="T" if kc == 15 else None)
            cb = ctx.op("scalar", lambda e, cc=cc:
                        e.copy(qs[:, 0:3], bndry[:, cc]),
                        deps=[prev_bnd[cc], qs_guard], inc="A")
            ce = ctx.op("scalar", lambda e, bank=bank:
                        e.copy(qs[:, 3:515], bank[:, :]), deps=mm, inc="A")
            guards[gk] = ce
            prev_bnd[cc] = ctx.op("vector", lambda e, cc=cc:
                                  e.tensor_copy(bndry[:, cc], qs[:, 512:515]),
                                  deps=ce, inc="V")
            cj = ctx.op("vector", lambda e, cc=cc:
                        e.tensor_scalar(conva, qs[:, 0:512],
                                        cw_sb[:, cwi, cc, 0:1],
                                        None, Op.mult, Op.bypass),
                        deps=[cb, ce], inc="V")
            for j in (1, 2, 3):
                cj = ctx.op("vector", lambda e, cc=cc, j=j:
                            e.scalar_tensor_tensor(conva, qs[:, j:j + 512],
                                                   cw_sb[:, cwi, cc, j:j + 1],
                                                   conva, Op.mult, Op.add),
                            deps=cj, inc="V")
            qs_guard = cj
            sl = ctx.op("scalar", lambda e:
                        e.activation(silf, conva, AF.Silu),
                        deps=[cj, silf_guard], inc="A")

            if which in ("q", "k"):
                sq = ctx.op("scalar", lambda e:
                            e.activation(sqb, silf, AF.Square), deps=sl,
                            inc="A")
                smm = ctx.op("tensor", lambda e:
                             e.matmul(PF[2][0:1, 0:512], onesb_sb[:, :], sqb,
                                      start=True, stop=True),
                             deps=[sq, sm_guard], inc="T")
                sv = ctx.op("scalar", lambda e:
                            e.activation(rrow[0:1, 0], PF[2][0:1, 0:512],
                                         AF.Sqrt), deps=[smm, rr_guard], inc="A")
                sm_guard = sv
                mxm = ctx.op("vector", lambda e:
                             e.tensor_single_scalar(rrow[0:1, 0], rrow[0:1, 0],
                                                    1e-6, Op.max),
                             deps=sv, inc="V")
                rv = ctx.op("vector", lambda e:
                            e.reciprocal(rrow[0:1, 0], rrow[0:1, 0]),
                            deps=mxm, inc="V")
                if which == "q":
                    bc = ctx.op("tensor", lambda e:
                                e.matmul(PF[3][:, 0:512], ones1_sb[:, :],
                                         rrow[0:1, 0], start=True, stop=True),
                                deps=[rv, bc_guard], inc="T")
                    nm = ctx.op("vector", lambda e:
                                e.tensor_tensor(silf, silf, PF[3][:, 0:512],
                                                Op.mult), deps=bc, inc="V")
                    bc_guard = nm
                    rr_guard = bc
                    el = ctx.dma("sync", ebt[:, cc % 2],
                                 S["eb_s"][cc * 128:(cc + 1) * 128, t0:t0 + 512],
                                 deps=el_guard[cc % 2],
                                 sem="EA" if cc % 2 == 0 else "EB")
                    qt = ctx.op("vector", lambda e, cc=cc:
                                e.tensor_tensor(stg, silf, ebt[:, cc % 2],
                                                Op.mult),
                                deps=[nm, el, ctx.all_of("St")], inc="V")
                    el_guard[cc % 2] = qt
                    silf_guard = qt
                    stg_guard = ctx.dma(
                        "scalar", S["qt_s"][cc * 128:(cc + 1) * 128, t0:t0 + 512],
                        stg, deps=qt, sem="St")
                else:
                    bc = ctx.op("tensor", lambda e:
                                e.matmul(PF[3][:, 0:512], ones1_sb[:, :],
                                         rrow[0:1, 0], start=True, stop=True),
                                deps=[rv, bc_guard], inc="T")
                    nm = ctx.op("vector", lambda e:
                                e.tensor_tensor(knf, silf, PF[3][:, 0:512],
                                                Op.mult), deps=bc, inc="V")
                    bc_guard = nm
                    rr_guard = bc
                    silf_guard = nm
                    el2 = ctx.dma("sync", ebt2[:, cc % 2],
                                  S["enb_s"][cc * 128:(cc + 1) * 128, t0:t0 + 512],
                                  deps=el_guard[cc % 2],
                                  sem="EA" if cc % 2 == 0 else "EB")
                    kb = ctx.op("vector", lambda e, cc=cc:
                                e.tensor_tensor(stg2, knf, ebt2[:, cc % 2],
                                                Op.mult),
                                deps=[nm, el2, ctx.all_of("St")], inc="V")
                    stg2_guard = ctx.dma(
                        "scalar", S["kb_s"][cc * 128:(cc + 1) * 128, t0:t0 + 512],
                        stg2, deps=kb, sem="St")
                    bc2 = ctx.op("tensor", lambda e, cc=cc:
                                 e.matmul(PF[3][:, 0:512], ones1_sb[:, :],
                                          brow_sb[0:1, cc], start=True, stop=True),
                                 deps=[bload[ti], bc_guard], inc="T")
                    nb = ctx.op("vector", lambda e:
                                e.tensor_tensor(knf, knf, PF[3][:, 0:512],
                                                Op.mult), deps=bc2, inc="V")
                    bc_guard = nb
                    el = ctx.dma("sync", ebt[:, cc % 2],
                                 S["eb_s"][cc * 128:(cc + 1) * 128, t0:t0 + 512],
                                 deps=kb, sem="EA" if cc % 2 == 0 else "EB")
                    kt = ctx.op("vector", lambda e, cc=cc:
                                e.tensor_tensor(stg, knf, ebt[:, cc % 2],
                                                Op.mult),
                                deps=[nb, el, ctx.all_of("St")], inc="V")
                    el_guard[cc % 2] = kt
                    stg_guard = ctx.dma(
                        "scalar", S["bkt_s"][cc * 128:(cc + 1) * 128, t0:t0 + 512],
                        stg, deps=kt, sem="St")
            else:
                bc = ctx.op("tensor", lambda e, cc=cc:
                            e.matmul(PF[3][:, 0:512], ones1_sb[:, :],
                                     brow_sb[0:1, cc], start=True, stop=True),
                            deps=[bload[ti], bc_guard, sl], inc="T")
                nm = ctx.op("vector", lambda e:
                            e.tensor_tensor(stg, silf, PF[3][:, 0:512], Op.mult),
                            deps=[bc, ctx.all_of("St")], inc="V")
                bc_guard = nm
                silf_guard = nm
                stg_guard = ctx.dma(
                    "scalar", S["vb_s"][cc * 128:(cc + 1) * 128, t0:t0 + 512],
                    stg, deps=nm, sem="St")
        if ti + 2 < NTL:
            xw.append(load(ti + 2, deps=[("T", ctx.cnt["T"])]))
        else:
            xw.append(None)
        if needs_beta:
            if ti + 1 < NTL:
                bload.append(ctx.dma("sync", brow_sb[:, :, :],
                                     S["betaf_s"][:, (ti + 1) * 512:(ti + 2) * 512]
                                     .rearrange("(o h) t -> o h t", o=1),
                                     deps=[("T", ctx.cnt["T"])], sem="BL"))
            else:
                bload.append(None)


SCAN_STOP = None
SCAN_NCHK = None

def emit_scan(ctx, ns):
    S = ns["S"]
    fb, bb, PF, PBF = ns["fb"], ns["bb"], ns["PF"], ns["PBF"]
    ident_sb, maskN_sb, maskU_sb = ns["ident_sb"], ns["maskN_sb"], ns["maskU_sb"]

    sp_qt = bb.take(2, 16, 128)
    sp_bkt = bb.take(2, 16, 128)
    sp_kb = bb.take(2, 16, 128)
    sp_vt = bb.take(2, 2048)
    sp_kbt = bb.take(2, 2048)
    kbt_b = bb.take(2048)
    Nt = bb.take(8, 128)
    NTt = bb.take(8, 128)
    Xa = bb.take(8, 128)
    Xb = bb.take(8, 128)
    XT_ = bb.take(5, 8, 128)
    XT = [XT_[:, i] for i in range(5)]
    r2 = bb.take(2, 8, 128)
    u2 = bb.take(8, 128)
    attnT = bb.take(8, 128)
    obuf = bb.take(2048)
    S_b = bb.take(16, 128)
    S_f = fb.take(16, 128)
    Stmp = fb.take(128)
    ebc_t = fb.take(2, 16, 2)

    z1 = ctx.op("vector", lambda e:
                e.memset(S_f.rearrange("p a b -> p (a b)"), 0.0), inc="V")
    z2 = ctx.op("vector", lambda e:
                e.memset(S_b.rearrange("p a b -> p (a b)"), 0.0), inc="V")
    zp0 = ctx.op("vector", lambda e: e.memset(PF[0][:, :], 0.0), inc="V")
    zp1 = ctx.op("vector", lambda e: e.memset(PF[1][:, :], 0.0), inc="V")

    nchk = SCAN_NCHK or NCHK
    NSPAN = nchk // 2

    def span_load(sp, deps=None):
        t0 = sp * 128
        pbuf = sp % 2
        sl = "SA" if pbuf == 0 else "SB"
        tl = "TA" if pbuf == 0 else "TB"
        m = []
        m.append(ctx.dma("sync", sp_qt[:, pbuf],
                         S["qt_s"][:, t0:t0 + 128]
                         .rearrange("(h p) t -> p h t", p=128), deps=deps, sem=sl))
        m.append(ctx.dma("sync", sp_bkt[:, pbuf],
                         S["bkt_s"][:, t0:t0 + 128]
                         .rearrange("(h p) t -> p h t", p=128), sem=sl))
        m.append(ctx.dma("sync", sp_kb[:, pbuf],
                         S["kb_s"][:, t0:t0 + 128]
                         .rearrange("(h p) t -> p h t", p=128), sem=sl))
        m.append(ctx.dma("scalar", sp_vt[:, pbuf], S["vb_s"][:, t0:t0 + 128],
                         transpose=True, deps=deps, sem=tl))
        m.append(ctx.dma("scalar", sp_kbt[:, pbuf], S["kb_s"][:, t0:t0 + 128],
                         transpose=True, sem=tl))
        m.append(ctx.dma("sync", ebc_t[:, pbuf],
                         S["ebc_s"][:, sp * 2:sp * 2 + 2]
                         .rearrange("(h p) c -> p h c", p=128), sem=sl))
        return m

    spl = [span_load(0, deps=[z1, z2])]
    gA = [zp0, zp1]
    gSQ = [None, None]
    gR = [None, None]
    gNT = None
    s_dep = {h: (z1 if True else None) for h in range(NH)}
    sf_dep = {h: z1 for h in range(NH)}
    prev_stt = z1

    for n in range(nchk):
        sp, cs = n // 2, n % 2
        pbuf = sp % 2
        if cs == 0 and sp + 1 < NSPAN:
            spl.append(span_load(sp + 1,
                                 deps=[("T", ctx.cnt["T"]), ("V", ctx.cnt["V"]),
                                       ("P", ctx.cnt["P"])]))
        ldep = spl[sp]
        qt = sp_qt[:, pbuf]
        bkt = sp_bkt[:, pbuf]
        kb = sp_kb[:, pbuf]
        vt = sp_vt[:, pbuf]
        kbt = sp_kbt[:, pbuf]
        kbtm = None
        if cs == 1:
            kbtm = ctx.op("scalar", lambda e, kbt=kbt:
                          e.copy(kbt_b[0:64, :], kbt[64:128, :]),
                          deps=ldep + [("T", ctx.cnt["T"])], inc="A")

        am = [None, None]
        for pi in range(NPAIR):
            bnk = pi // 4
            col = (pi % 4) * 128
            h0, h1 = 2 * pi, 2 * pi + 1
            dd = ldep + [gA[bnk]] if pi % 4 == 0 else None
            ctx.op("tensor", lambda e, bnk=bnk, col=col, h0=h0, cs=cs, bkt=bkt, kb=kb:
                   e.matmul(PF[bnk][0:64, col:col + 64],
                            bkt[:, h0, cs * 64:(cs + 1) * 64],
                            kb[:, h0, cs * 64:(cs + 1) * 64],
                            start=True, stop=True), deps=dd)
            am[bnk] = ctx.op("tensor", lambda e, bnk=bnk, col=col, h1=h1, cs=cs, bkt=bkt, kb=kb:
                             e.matmul(PF[bnk][64:128, col + 64:col + 128],
                                      bkt[:, h1, cs * 64:(cs + 1) * 64],
                                      kb[:, h1, cs * 64:(cs + 1) * 64],
                                      start=True, stop=True),
                             inc="T" if pi % 4 == 3 else None)
        if SCAN_STOP == 'A':
            continue
        nmk = [None, None]
        for bnk in range(2):
            nmk[bnk] = ctx.op("vector", lambda e, bnk=bnk:
                              e.tensor_tensor(Nt[:, 4 * bnk:4 * bnk + 4]
                                              .rearrange("p a b -> p (a b)"),
                                              PF[bnk][:, :], maskN_sb[:, :],
                                              Op.mult), deps=am[bnk], inc="V")
        gA = [nmk[0], nmk[1]]
        if SCAN_STOP == 'N':
            continue
        tt = None
        for pi in range(NPAIR):
            tt = ctx.op("tensor", lambda e, pi=pi:
                        e.transpose(PBF[0][:, pi * 128:(pi + 1) * 128],
                                    Nt[:, pi], ident_sb[:, :]),
                        deps=[nmk[pi // 4], gNT] if pi == 0 else
                        ([nmk[1]] if pi == 4 else None),
                        inc="T" if pi == 7 else None)
        gNT = ctx.op("scalar", lambda e:
                     e.copy(NTt.rearrange("p a b -> p (a b)"), PBF[0][:, :]),
                     deps=tt, inc="A")

        if SCAN_STOP == 'NT':
            continue
        sq_steps = [
            (NTt, Nt, Xa),
            (Nt, NTt, XT[0]),
            (XT[0], Xa, Xb),
            (Xa, XT[0], XT[1]),
            (XT[1], Xb, Xa),
            (Xb, XT[1], XT[2]),
            (XT[2], Xa, Xb),
            (Xa, XT[2], XT[3]),
            (Xb, XT[3], XT[4]),
        ]
        prev_ev = [gNT, gNT]
        for si, (lhs, rhs, dst) in enumerate(sq_steps):
            smm = [None, None]
            for pi in range(NPAIR):
                bnk = 2 + pi // 4
                col = (pi % 4) * 128
                dd = [prev_ev[0], prev_ev[1], gSQ[pi // 4]] if pi % 4 == 0 else None
                smm[pi // 4] = ctx.op(
                    "tensor", lambda e, bnk=bnk, col=col, pi=pi, lhs=lhs, rhs=rhs:
                    e.matmul(PF[bnk][:, col:col + 128], lhs[:, pi], rhs[:, pi],
                             start=True, stop=True),
                    deps=dd, inc="T" if pi % 4 == 3 else None)
            ev = [None, None]
            for bnk in range(2):
                if si % 2 == 0:
                    ev[bnk] = ctx.op("scalar", lambda e, bnk=bnk, dst=dst:
                                     e.copy(dst[:, 4 * bnk:4 * bnk + 4]
                                            .rearrange("p a b -> p (a b)"),
                                            PF[2 + bnk][:, :]),
                                     deps=smm[bnk], inc="A")
                else:
                    ev[bnk] = ctx.op("vector", lambda e, bnk=bnk, dst=dst:
                                     e.tensor_copy(dst[:, 4 * bnk:4 * bnk + 4]
                                                   .rearrange("p a b -> p (a b)"),
                                                   PF[2 + bnk][:, :]),
                                     deps=smm[bnk], inc="V")
            gSQ = ev
            prev_ev = ev

        if SCAN_STOP == 'SQ':
            continue
        km = [None, None]
        for pi in range(NPAIR):
            bnk = 4 + pi // 4
            col = (pi % 4) * 128
            h0, h1 = 2 * pi, 2 * pi + 1
            ctx.op("tensor", lambda e, bnk=bnk, col=col, h0=h0, cs=cs, bkt=bkt:
                   e.matmul(PF[bnk][0:64, col:col + 128],
                            bkt[:, h0, cs * 64:(cs + 1) * 64], S_b[:, h0],
                            start=True, stop=True),
                   deps=[gR[pi // 4], s_dep[h0], s_dep[h1]])
            km[pi // 4] = ctx.op(
                "tensor", lambda e, bnk=bnk, col=col, h1=h1, cs=cs, bkt=bkt:
                e.matmul(PF[bnk][64:128, col:col + 128],
                         bkt[:, h1, cs * 64:(cs + 1) * 64], S_b[:, h1],
                         start=True, stop=True),
                inc="T" if pi % 4 == 3 else None)
        if SCAN_STOP == 'KS':
            continue
        rv = [None, None]
        for pi in range(NPAIR):
            bnk = 4 + pi // 4
            col = (pi % 4) * 128
            h0, h1 = 2 * pi, 2 * pi + 1
            ctx.op("vector", lambda e, bnk=bnk, col=col, h0=h0, pi=pi, cs=cs, vt=vt:
                   e.tensor_tensor(r2[:, 0, pi][0:64, :],
                                   vt[cs * 64:cs * 64 + 64, h0 * 128:(h0 + 1) * 128],
                                   PF[bnk][0:64, col:col + 128], Op.subtract),
                   deps=km[pi // 4] if pi % 4 == 0 else None)
            rv[pi // 4] = ctx.op(
                "vector", lambda e, bnk=bnk, col=col, h1=h1, pi=pi, cs=cs, vt=vt:
                e.tensor_tensor(r2[:, 0, pi][64:128, :],
                                vt[cs * 64:cs * 64 + 64, h1 * 128:(h1 + 1) * 128],
                                PF[bnk][64:128, col:col + 128], Op.subtract),
                inc="V" if pi % 4 == 3 else None)
        gR = rv
        if SCAN_STOP == 'RHS':
            continue
        app_lhs = [XT[4], XT[3], XT[2], XT[1], XT[0], NTt]
        sqdone = list(prev_ev)
        rsrc, rdst = 0, 1
        prev_rv = rv
        for lvl in range(6):
            lmm = [None, None]
            for pi in range(NPAIR):
                bnk = 4 + pi // 4
                col = (pi % 4) * 128
                dd = ([prev_rv[0], prev_rv[1], gR[pi // 4]] + sqdone) \
                    if pi % 4 == 0 else None
                lmm[pi // 4] = ctx.op(
                    "tensor",
                    lambda e, bnk=bnk, col=col, pi=pi, lhs=app_lhs[lvl], rs=rsrc:
                    e.matmul(PF[bnk][:, col:col + 128], lhs[:, pi], r2[:, rs, pi],
                             start=True, stop=True),
                    deps=dd, inc="T" if pi % 4 == 3 else None)
            nev = [None, None]
            for bnk in range(2):
                nev[bnk] = ctx.op(
                    "vector", lambda e, bnk=bnk, rs=rsrc, rd=rdst:
                    e.tensor_tensor(r2[:, rd, 4 * bnk:4 * bnk + 4]
                                    .rearrange("p a b -> p (a b)"),
                                    PF[4 + bnk][:, :],
                                    r2[:, rs, 4 * bnk:4 * bnk + 4]
                                    .rearrange("p a b -> p (a b)"), Op.add),
                    deps=lmm[bnk], inc="V")
            gR = nev
            prev_rv = nev
            rsrc, rdst = rdst, rsrc
        u_idx = rsrc
        u_mark = prev_rv

        if SCAN_STOP == 'APP':
            continue
        atm = [None, None]
        for pi in range(NPAIR):
            bnk = pi // 4
            col = (pi % 4) * 128
            h0, h1 = 2 * pi, 2 * pi + 1
            dd = [gA[0], gA[1]] if pi == 0 else None
            ctx.op("tensor", lambda e, bnk=bnk, col=col, h0=h0, cs=cs, kb=kb, qt=qt:
                   e.matmul(PF[bnk][0:64, col:col + 64],
                            kb[:, h0, cs * 64:(cs + 1) * 64],
                            qt[:, h0, cs * 64:(cs + 1) * 64],
                            start=True, stop=True), deps=dd)
            atm[bnk] = ctx.op("tensor", lambda e, bnk=bnk, col=col, h1=h1, cs=cs, kb=kb, qt=qt:
                              e.matmul(PF[bnk][64:128, col + 64:col + 128],
                                       kb[:, h1, cs * 64:(cs + 1) * 64],
                                       qt[:, h1, cs * 64:(cs + 1) * 64],
                                       start=True, stop=True),
                              inc="T" if pi % 4 == 3 else None)
        amk = [None, None]
        for bnk in range(2):
            amk[bnk] = ctx.op("vector", lambda e, bnk=bnk:
                              e.tensor_tensor(attnT[:, 4 * bnk:4 * bnk + 4]
                                              .rearrange("p a b -> p (a b)"),
                                              PF[bnk][:, :], maskU_sb[:, :],
                                              Op.mult), deps=atm[bnk], inc="V")
        gA = amk

        if SCAN_STOP == 'ATT':
            continue
        om = [None, None]
        for pi in range(NPAIR):
            bnk = 2 + pi // 4
            col = (pi % 4) * 128
            h0, h1 = 2 * pi, 2 * pi + 1
            dd = [gSQ[0], gSQ[1], amk[pi // 4], u_mark[0], u_mark[1]] \
                if pi % 4 == 0 else None
            ctx.op("tensor", lambda e, bnk=bnk, col=col, h0=h0, cs=cs, qt=qt:
                   e.matmul(PF[bnk][0:64, col:col + 128],
                            qt[:, h0, cs * 64:(cs + 1) * 64], S_b[:, h0],
                            start=True, stop=False), deps=dd)
            ctx.op("tensor", lambda e, bnk=bnk, col=col, pi=pi, ui=u_idx:
                   e.matmul(PF[bnk][0:64, col:col + 128],
                            attnT[:, pi][:, 0:64], r2[:, ui, pi],
                            start=False, stop=True))
            ctx.op("tensor", lambda e, bnk=bnk, col=col, h1=h1, cs=cs, qt=qt:
                   e.matmul(PF[bnk][64:128, col:col + 128],
                            qt[:, h1, cs * 64:(cs + 1) * 64], S_b[:, h1],
                            start=True, stop=False))
            om[pi // 4] = ctx.op(
                "tensor", lambda e, bnk=bnk, col=col, pi=pi, ui=u_idx:
                e.matmul(PF[bnk][64:128, col:col + 128],
                         attnT[:, pi][:, 64:128], r2[:, ui, pi],
                         start=False, stop=True),
                inc="T" if pi % 4 == 3 else None)
        oev = None
        for pi in range(NPAIR):
            bnk = 2 + pi // 4
            col = (pi % 4) * 128
            h0, h1 = 2 * pi, 2 * pi + 1
            dd = [om[pi // 4]] if pi % 4 == 0 else None
            if pi == 0 and cs == 0:
                dd = [om[0], ctx.all_of("St")]
            ctx.op("scalar", lambda e, bnk=bnk, col=col, h0=h0, cs=cs:
                   e.copy(obuf[cs * 64:cs * 64 + 64, h0 * 128:(h0 + 1) * 128],
                          PF[bnk][0:64, col:col + 128]),
                   deps=dd)
            oev = ctx.op("scalar", lambda e, bnk=bnk, col=col, h1=h1, cs=cs:
                         e.copy(obuf[cs * 64:cs * 64 + 64,
                                     h1 * 128:(h1 + 1) * 128],
                                PF[bnk][64:128, col:col + 128]),
                         inc="A" if pi == 7 else None)
        gSQ = [oev, oev]

        if SCAN_STOP == 'O':
            continue
        for rnd in range(2):
            if rnd == 0:
                u2m = ctx.op("vector", lambda e, ui=u_idx:
                             e.tensor_copy(u2.rearrange("p a b -> p (a b)")[0:64, :],
                                           r2[:, ui].rearrange("p a b -> p (a b)")
                                           [64:128, :]),
                             deps=[u_mark[0], u_mark[1],
                                   ("T", ctx.cnt["T"])], inc="V")
            sam = [None, None]
            for hh in range(8):
                h = rnd * 8 + hh
                pi = h // 2
                bnk = 4 + hh // 4
                col = (hh % 4) * 128
                half = h % 2
                ksrc = kbt if cs == 0 else kbt_b
                dd = [gR[0], gR[1], kbtm, u2m] if hh == 0 else None
                sam[hh // 4] = ctx.op(
                    "tensor",
                    lambda e, bnk=bnk, col=col, h=h, pi=pi, half=half, cs=cs,
                    ui=u_idx, ksrc=ksrc:
                    e.matmul(PF[bnk][:, col:col + 128],
                             ksrc[0:64, h * 128:(h + 1) * 128],
                             (r2[:, ui, pi] if half == 0 else u2[:, pi])[0:64, :],
                             start=True, stop=True),
                    deps=dd, inc="T" if hh % 4 == 3 else None)
            upd = [None, None]
            for hh in range(8):
                h = rnd * 8 + hh
                bnk = 4 + hh // 4
                col = (hh % 4) * 128
                stm = ctx.op("vector",
                             lambda e, bnk=bnk, col=col, h=h, pbuf=pbuf, cs=cs:
                             e.tensor_scalar(Stmp, PF[bnk][:, col:col + 128],
                                             ebc_t[:, pbuf, h, cs:cs + 1], None,
                                             Op.mult, Op.bypass),
                             deps=([sam[hh // 4]] if hh % 4 == 0 else [])
                             + [sf_dep[h], prev_stt], inc="V")
                sf_dep[h] = prev_stt = ctx.op(
                    "vector", lambda e, h=h, pbuf=pbuf, cs=cs:
                    e.scalar_tensor_tensor(S_f[:, h], S_f[:, h],
                                           ebc_t[:, pbuf, h, cs:cs + 1],
                                           Stmp, Op.mult, Op.add),
                    deps=[s_dep[h], stm], inc="V")
                upd[hh // 4] = sf_dep[h]
                s_dep[h] = ctx.op("gpsimd", lambda e, h=h:
                                  e.tensor_copy(S_b[:, h], S_f[:, h]),
                                  deps=sf_dep[h], inc="P")
            gR = upd

        if SCAN_STOP == 'SADD':
            continue
        if cs == 1:
            ctx.dma("sync", S["o_s"][sp * 128:(sp + 1) * 128, :],
                    obuf, deps=oev, sem="St")


def emit_epi(ctx, ns):
    W, S, out = ns["W"], ns["S"], ns["out"]
    fb, bb, PF = ns["fb"], ns["bb"], ns["PF"]
    ones1_sb, rmsbc_sb = ns["ones1_sb"], ns["rmsbc_sb"]

    wo_sb = bb.take(16, 2048)
    otl = bb.take(1, 4, 2048)
    gtl = bb.take(1, 4, 2048)
    ogt = bb.take(4, 2048)
    ogf = bb.take(16, 512)
    xtl = fb.take(4, 2048)
    sums = fb.take(4, 16)
    rmsr = fb.take(4, 16)
    tmpf = fb.take(256)
    rrow = fb.take(128)

    wld = ctx.dma("gpsimd", wo_sb, W["Wo"].rearrange("(c p) n -> p c n", p=128), sem="WQ")
    rl = ctx.dma("sync", rrow[0:1, :], W["rmsw"].rearrange("(a d) -> a d", a=1))
    bm = ctx.op("tensor", lambda e:
                e.matmul(PF[0][:, 0:128], ones1_sb[:, :], rrow[0:1, :],
                         start=True, stop=True), deps=[rl], inc="T")
    rb = ctx.op("vector", lambda e:
                e.tensor_copy(rmsbc_sb[:, :], PF[0][:, 0:128]), deps=bm, inc="V")

    guards = {i: None for i in range(6)}
    guards[0] = rb
    oload = [None] * (NOTL + 2)
    gload = [None] * (NOTL + 2)
    tg = HALO
    oload[0] = ctx.dma("sync", otl[:, 0],
                       S["o_s"][tg:tg + 512, :]
                       .rearrange("(g p) d -> p g d", p=128), sem="OA")
    gload[0] = ctx.dma("sync", gtl[:, 0],
                       S["gate_s"][tg:tg + 512, :]
                       .rearrange("(g p) d -> p g d", p=128), sem="OA")

    prev_outst = None
    prev_ogst = None
    sq_guard = None
    for ti in range(NOTL):
        p = ti % 2
        tg = HALO + ti * 512
        xl = ctx.dma("sync", xtl, ns["x_ext"][3 + tg:3 + tg + 512, :]
                     .rearrange("(g p) d -> p g d", p=128),
                     deps=[("V", ctx.cnt["V"]), ctx.all_of("St")] if ti else None,
                     sem="XL")
        ogm = None
        for g in range(4):
            for h in range(NH):
                sqm = ctx.op("scalar", lambda e, p=p, g=g, h=h:
                             e.activation(tmpf[:, 128:256],
                                          otl[:, 0, g, h * 128:(h + 1) * 128],
                                          AF.Square,
                                          accum_out=sums[:, g, h:h + 1]),
                             deps=[gload[ti], sq_guard]
                             if (g == 0 and h == 0) else sq_guard, inc="A")
                sq_guard = sqm
            sa = ctx.op("scalar", lambda e, g=g:
                        e.activation(rmsr[:, g], sums[:, g], AF.Sqrt,
                                     scale=1.0 / 128, bias=1e-6),
                        deps=sq_guard, inc="A")
            rv = ctx.op("vector", lambda e, g=g:
                        e.reciprocal(rmsr[:, g], rmsr[:, g]), deps=sa, inc="V")
            for h in range(NH):
                dd = None
                if h == 0:
                    dd = [rv, gload[ti]]
                    if g == 0:
                        dd.append(ctx.all_of("St"))
                stm1 = ctx.op("vector", lambda e, p=p, g=g, h=h:
                              e.scalar_tensor_tensor(
                                  tmpf[:, 0:128],
                                  otl[:, 0, g, h * 128:(h + 1) * 128],
                                  rmsr[:, g, h:h + 1],
                                  gtl[:, 0, g, h * 128:(h + 1) * 128],
                                  Op.mult, Op.mult),
                              deps=(dd or []) + [ogm], inc="V")
                ogm = ctx.op("vector", lambda e, g=g, h=h:
                             e.tensor_tensor(ogt[:, g, h * 128:(h + 1) * 128],
                                             tmpf[:, 0:128], rmsbc_sb[:, :],
                                             Op.mult),
                             deps=stm1, inc="V")
        ogst = ctx.dma("scalar", S["og_s"][ti * 512:(ti + 1) * 512, :]
                       .rearrange("(g p) d -> p g d", p=128), ogt,
                       deps=ogm, sem="St")
        prev_ogst = ogst
        tl = None
        for cc in range(KC):
            tl = ctx.dma("sync", ogf[:, cc],
                         S["og_s"][ti * 512:(ti + 1) * 512, cc * 128:(cc + 1) * 128],
                         transpose=True, sem="TL",
                         deps=[ogst, ("T", ctx.cnt["T"])] if cc == 0 else None)
        stm = None
        for g in range(4):
            for hc in range(4):
                bk = 1 + (g * 4 + hc) % 2
                mm = None
                for cc in range(KC):
                    mm = ctx.op("tensor",
                                lambda e, g=g, hc=hc, cc=cc, bk=bk:
                                e.matmul(PF[bk][:, :],
                                         ogf[:, cc, g * 128:(g + 1) * 128],
                                         wo_sb[:, cc, hc * 512:(hc + 1) * 512],
                                         start=(cc == 0), stop=(cc == 15)),
                                deps=[tl, xl, ctx.all_of("WQ"), guards[bk]]
                                if cc == 0 else None,
                                inc="T" if cc == 15 else None)
                guards[bk] = ctx.op(
                    "vector", lambda e, g=g, hc=hc, bk=bk:
                    e.tensor_tensor(xtl[:, g, hc * 512:(hc + 1) * 512],
                                    xtl[:, g, hc * 512:(hc + 1) * 512],
                                    PF[bk][:, :], Op.add),
                    deps=mm, inc="V")
            stm = guards[1 + (g * 4 + 3) % 2]
        prev_outst = ctx.dma("scalar", out[ti * 512:(ti + 1) * 512, :]
                             .rearrange("(g p) d -> p g d", p=128), xtl,
                             deps=[guards[1], guards[2]], sem="St")
        if ti + 1 < NOTL:
            tg2 = HALO + (ti + 1) * 512
            oload[ti + 1] = ctx.dma("sync", otl[:, 0],
                                    S["o_s"][tg2:tg2 + 512, :]
                                    .rearrange("(g p) d -> p g d", p=128),
                                    deps=[("V", ctx.cnt["V"]),
                                          ("A", ctx.cnt["A"])], sem="OA")
            gload[ti + 1] = ctx.dma("sync", gtl[:, 0],
                                    S["gate_s"][tg2:tg2 + 512, :]
                                    .rearrange("(g p) d -> p g d", p=128),
                                    sem="OA")


def emit_ffn(ctx, ns):
    W, S, out = ns["W"], ns["S"], ns["out"]
    fb, bb, PF, PBF = ns["fb"], ns["bb"], ns["PF"], ns["PBF"]
    comb_sb, sg2_sb, ident_sb = ns["comb_sb"], ns["sg2_sb"], ns["ident_sb"]

    xnfm = bb.take(2, 16, 512)
    gupT = bb.take(16, 1536)
    dwnT = bb.take(6, 2048)
    actb = bb.take(6, 512)
    rwb = bb.take(2048)
    rwT = bb.take(16, 8)
    sgb = bb.take(2048)
    sgT = bb.take(16, 1)
    curf = fb.take(4, 2048)
    silf = fb.take(512)
    lgf = fb.take(4, 8)
    m1f = fb.take(4, 1)
    m2f = fb.take(4, 1)
    zf = fb.take(4, 8)
    gef = fb.take(4, 8)

    rld = ctx.dma("gpsimd", rwb[0:8, :], W["rw"], sem="WQ")
    sld = ctx.dma("gpsimd", sgb[0:1, :], W["sgw"], sem="WQ")
    tvm = None
    for kc in range(KC):
        tvm = ctx.op("tensor", lambda e, kc=kc:
                     e.transpose(PBF[0][:, kc * 8:(kc + 1) * 8],
                                 rwb[0:8, kc * 128:(kc + 1) * 128],
                                 ident_sb[0:8, 0:8]),
                     deps=ctx.all_of("WQ") if kc == 0 else None,
                     inc="T" if kc == 15 else None)
    rv1 = ctx.op("vector", lambda e:
                 e.tensor_copy(rwT.rearrange("p a b -> p (a b)"),
                               PBF[0][:, 0:128]), deps=tvm, inc="V")
    tvm2 = None
    for kc in range(KC):
        tvm2 = ctx.op("tensor", lambda e, kc=kc:
                      e.transpose(PBF[1][:, kc * 2:kc * 2 + 1],
                                  sgb[0:1, kc * 128:(kc + 1) * 128],
                                  ident_sb[0:1, 0:1]),
                      deps=[ctx.all_of("WQ"), rv1] if kc == 0 else None,
                      inc="T" if kc == 15 else None)
    rv2 = ctx.op("vector", lambda e:
                 e.tensor_copy(sgT,
                               PBF[1][:, 0:32].rearrange("p (a b) -> p a b", b=2)[:, :, 0:1]),
                 deps=tvm2, inc="V")

    load = xn_loader(ctx, ns, xnfm)
    xw = [load(1 + 0), load(1 + 1, deps=[("T", ctx.cnt["T"])])]
    g0, g1 = None, None
    for ti in range(NOTL):
        p = (1 + ti) % 2
        lm = None
        sm = None
        for g in range(4):
            for kc in range(KC):
                lm = ctx.op("tensor", lambda e, g=g, kc=kc, p=p:
                            e.matmul(PF[0][:, g * 8:(g + 1) * 8],
                                     xnfm[:, p, kc, g * 128:(g + 1) * 128],
                                     rwT[:, kc],
                                     start=(kc == 0), stop=(kc == 15)),
                            deps=[xw[ti], rv1, g0]
                            if (g == 0 and kc == 0) else None,
                            inc="T" if kc == 15 else None)
            for kc in range(KC):
                sm = ctx.op("tensor", lambda e, g=g, kc=kc, p=p:
                            e.matmul(PF[1][:, g:g + 1],
                                     xnfm[:, p, kc, g * 128:(g + 1) * 128],
                                     sgT[:, kc],
                                     start=(kc == 0), stop=(kc == 15)),
                            deps=[rv2, g1] if (g == 0 and kc == 0) else None,
                            inc="T" if (kc == 15 and g == 3) else None)
        gl = None
        for g in range(4):
            gl = ctx.op("vector", lambda e, g=g:
                        e.tensor_copy(lgf[:, g], PF[0][:, g * 8:(g + 1) * 8]),
                        deps=[lm, gl] if g == 0 else gl, inc="V")
            gl = ctx.op("vector", lambda e, g=g:
                        e.tensor_reduce(m1f[:, g], lgf[:, g], AX.X, Op.max),
                        deps=gl, inc="V")
            gl = ctx.op("vector", lambda e, g=g:
                        e.tensor_scalar(m1f[:, g], m1f[:, g], -1.0, None,
                                        Op.mult, Op.bypass), deps=gl, inc="V")
            az = ctx.op("scalar", lambda e, g=g:
                        e.activation(zf[:, g], lgf[:, g], AF.Exp,
                                     bias=m1f[:, g]), deps=gl, inc="A")
            gl = ctx.op("vector", lambda e, g=g:
                        e.tensor_single_scalar(gef[:, g], zf[:, g], 1.0,
                                               Op.is_ge), deps=az, inc="V")
            gl = ctx.op("vector", lambda e, g=g:
                        e.scalar_tensor_tensor(gef[:, g], gef[:, g], -2.0,
                                               zf[:, g], Op.mult, Op.add),
                        deps=gl, inc="V")
            gl = ctx.op("vector", lambda e, g=g:
                        e.tensor_reduce(m2f[:, g], gef[:, g], AX.X, Op.max),
                        deps=gl, inc="V")
            gl = ctx.op("vector", lambda e, g=g:
                        e.tensor_scalar(gef[:, g], zf[:, g], m2f[:, g], None,
                                        Op.is_ge, Op.bypass), deps=gl, inc="V")
            gl = ctx.op("vector", lambda e, g=g:
                        e.tensor_single_scalar(m2f[:, g], m2f[:, g], 1.0,
                                               Op.add), deps=gl, inc="V")
            gl = ctx.op("vector", lambda e, g=g:
                        e.reciprocal(m2f[:, g], m2f[:, g]), deps=gl, inc="V")
            gl = ctx.op("vector", lambda e, g=g, ti=ti:
                        e.tensor_tensor(gef[:, g], gef[:, g], zf[:, g],
                                        Op.mult), deps=gl, inc="V")
            gl = ctx.op("vector", lambda e, g=g, ti=ti:
                        e.tensor_scalar(comb_sb[:, ti, g], gef[:, g],
                                        m2f[:, g], None, Op.mult, Op.bypass),
                        deps=gl, inc="V")
        g0 = gl
        sgl = ctx.op("scalar", lambda e, ti=ti:
                     e.activation(sg2_sb[:, ti], PF[1][:, 0:4], AF.Sigmoid),
                     deps=sm, inc="A")
        g1 = sgl
        if ti + 2 < NOTL:
            xw.append(load(1 + ti + 2, deps=[("T", ctx.cnt["T"])]))
        else:
            xw.append(None)

    wait_cast = ("Lg", ctx.cnt["Lg"])

    def wload(le, half):
        marks = []
        if le < E:
            gsrc, usrc, dsrc = S["gup_b"][le], S["gup_b"][le], S["dwn_b"][le]
            gbase, ubase, ibase = 0, IMOE, 0
        else:
            sl = le - E
            gsrc, usrc, dsrc = S["shg_b"], S["shu_b"], S["shd_b"]
            gbase = ubase = sl * IMOE
            ibase = sl * IMOE
        i0 = half * 768
        ni = 768 if half == 0 else 640
        nic = ni // 128
        for kc in range(KC):
            for rr in range(0, ni, 512):
                w = min(512, ni - rr)
                marks.append(ctx.dma(
                    "sync", gupT[:, kc, rr:rr + w],
                    gsrc[gbase + i0 + rr:gbase + i0 + rr + w,
                         kc * 128:(kc + 1) * 128], transpose=True, sem="WL",
                    deps=[wait_cast, ("T", ctx.cnt["T"])]
                    if not marks else None))
                marks.append(ctx.dma(
                    "sync", gupT[:, kc, 768 + rr:768 + rr + w],
                    usrc[ubase + i0 + rr:ubase + i0 + rr + w,
                         kc * 128:(kc + 1) * 128], transpose=True, sem="WL"))
        for ic in range(nic):
            if le < E:
                src = dsrc[:, i0 + ic * 128:i0 + (ic + 1) * 128]
            else:
                src = dsrc[:, ibase + i0 + ic * 128:ibase + i0 + (ic + 1) * 128]
            marks.append(ctx.dma("sync", dwnT[:, ic], src, transpose=True,
                                 sem="WL"))
        return marks, nic

    prev_acc = [None] * NOTL
    gu_guard = [g0, g1]
    dn_guard = [None, None]
    ffn_silf_guard = None
    for le in range(E + 4):
        for half in range(2):
            wm, nic = wload(le, half)
            xw2 = [load(1 + 0, deps=[("T", ctx.cnt["T"])]),
                   load(1 + 1, deps=[("T", ctx.cnt["T"])])]
            for ti in range(NOTL):
                p = (1 + ti) % 2
                act_marks = []
                for j in range(nic):
                    gm = None
                    for kc in range(KC):
                        gm = ctx.op("tensor", lambda e, j=j, kc=kc, p=p:
                                    e.matmul(PF[0][:, :],
                                             gupT[:, kc, j * 128:(j + 1) * 128],
                                             xnfm[:, p, kc],
                                             start=(kc == 0), stop=(kc == 15)),
                                    deps=(wm + [xw2[ti], gu_guard[0]])
                                    if kc == 0 else None,
                                    inc="T" if kc == 15 else None)
                    um = None
                    for kc in range(KC):
                        um = ctx.op("tensor", lambda e, j=j, kc=kc, p=p:
                                    e.matmul(PF[1][:, :],
                                             gupT[:, kc,
                                                  768 + j * 128:768 + (j + 1) * 128],
                                             xnfm[:, p, kc],
                                             start=(kc == 0), stop=(kc == 15)),
                                    deps=[gu_guard[1]] if kc == 0 else None,
                                    inc="T" if kc == 15 else None)
                    sa = ctx.op("scalar", lambda e:
                                e.activation(silf, PF[0][:, :], AF.Silu),
                                deps=[gm, ffn_silf_guard], inc="A")
                    gu_guard[0] = sa
                    av = ctx.op("vector", lambda e, j=j:
                                e.tensor_tensor(actb[:, j], silf, PF[1][:, :],
                                                Op.mult), deps=[sa, um], inc="V")
                    gu_guard[1] = av
                    ffn_silf_guard = av
                    act_marks.append(av)
                ev = None
                for g in range(4):
                    for hc in range(4):
                        bk = 2 + (g * 4 + hc) % 2
                        mm = None
                        for ic in range(nic):
                            mm = ctx.op(
                                "tensor",
                                lambda e, g=g, hc=hc, ic=ic, bk=bk, nic=nic:
                                e.matmul(PF[bk][:, :],
                                         actb[:, ic, g * 128:(g + 1) * 128],
                                         dwnT[:, ic, hc * 512:(hc + 1) * 512],
                                         start=(ic == 0), stop=(ic == nic - 1)),
                                deps=(act_marks + [dn_guard[bk - 2]])
                                if ic == 0 else None,
                                inc="T" if ic == nic - 1 else None)
                        if le < E:
                            scl = comb_sb[:, ti, g, le:le + 1]
                        else:
                            scl = sg2_sb[:, ti, g:g + 1]
                        ev = ctx.op(
                            "scalar",
                            lambda e, g=g, hc=hc, bk=bk, scl=scl:
                            e.activation(curf[:, g, hc * 512:(hc + 1) * 512],
                                         PF[bk][:, :], AF.Copy, scale=scl),
                            deps=[mm, ctx.all_of("G")], inc="A")
                        dn_guard[bk - 2] = ev
                prev_acc[ti] = ctx.dma(
                    "gpsimd", out[ti * 512:(ti + 1) * 512, :]
                    .rearrange("(g p) d -> p g d", p=128), curf,
                    deps=ev, sem="G", accum=True)
                if ti + 2 < NOTL:
                    xw2.append(load(1 + ti + 2, deps=[("T", ctx.cnt["T"])]))
                else:
                    xw2.append(None)


def _host_consts():
    tril = np.tril(np.ones((CHK, CHK), np.float32), -1)
    triu = np.triu(np.ones((CHK, CHK), np.float32))
    mN = np.zeros((128, 512), np.float32)
    mU = np.zeros((128, 512), np.float32)
    for j in range(4):
        mN[0:64, j * 128:j * 128 + 64] = -tril
        mN[64:128, j * 128 + 64:j * 128 + 128] = -tril
        mU[0:64, j * 128:j * 128 + 64] = triu
        mU[64:128, j * 128 + 64:j * 128 + 128] = triu
    ident = np.eye(128, dtype=ml_dtypes.bfloat16)
    return ident, mN, mU


NAMEMAP = {"Wq": "L0_attn_Wq", "Wk": "L0_attn_Wk", "Wv": "L0_attn_Wv",
           "Wo": "L0_attn_Wo", "Wad": "L0_attn_Wad", "Wau": "L0_attn_Wau",
           "Wb": "L0_attn_Wb", "Wgd": "L0_attn_Wgd", "Wgu": "L0_attn_Wgu",
           "cqw": "L0_attn_conv_q_w", "ckw": "L0_attn_conv_k_w",
           "cvw": "L0_attn_conv_v_w", "rmsw": "L0_attn_rms_w",
           "rw": "L0_ffn_router_weight", "gup": "L0_ffn_gate_up_proj",
           "dwn": "L0_ffn_down_proj", "shg": "L0_ffn_sh_gate_w",
           "shu": "L0_ffn_sh_up_w", "shd": "L0_ffn_sh_down_w",
           "sgw": "L0_ffn_shared_gate_w"}


def make_in_maps(inputs):
    x = np.ascontiguousarray(inputs["L0_attn_x"])
    ident, mN, mU = _host_consts()
    weights = {k: np.ascontiguousarray(inputs[v]) for k, v in NAMEMAP.items()}
    in_maps = []
    for c in range(8):
        b, q = c // 4, c % 4
        t0 = q * TOWN
        ext0 = t0 - HALO - 3
        if ext0 < 0:
            xe = np.concatenate([np.zeros((-ext0, D), np.float32),
                                 x[b, :t0 + TOWN]], axis=0)
        else:
            xe = x[b, ext0:t0 + TOWN]
        m = {"x_ext": np.ascontiguousarray(xe), "identb": ident,
             "maskN": mN, "maskU": mU}
        m.update(weights)
        in_maps.append(m)
    return in_maps


def kernel(**inputs):
    in_maps = make_in_maps(inputs)
    nc, stack = build_program()
    try:
        res = run_bass_kernel_spmd(nc, in_maps, core_ids=list(range(8)))
    finally:
        stack.close()
    outs = [res.results[c]["out"] for c in range(8)]
    return np.stack([np.concatenate(outs[0:4], 0),
                     np.concatenate(outs[4:8], 0)], 0)


# version 02
# speedup vs baseline: 1.0362x; 1.0362x over previous
import numpy as np
import ml_dtypes
import concourse.bass as bass
import concourse.mybir as mybir
from concourse.alu_op_type import AluOpType as Op
from concourse.bass_utils import run_bass_kernel_spmd

F32 = mybir.dt.float32
BF16 = mybir.dt.bfloat16
AF = mybir.ActivationFunctionType
AX = mybir.AxisListType

B, T, D = 2, 16384, 2048
NH, DK, DV, CHK = 16, 128, 128, 64
TOWN, HALO = 4096, 512
TEXT = TOWN + HALO
NTL = TEXT // 512
NOTL = TOWN // 512
NCHK = TEXT // CHK
KC = D // 128
NPAIR = NH // 2
E, IMOE, ISH = 8, 1408, 5632

ENGINES = ("tensor", "vector", "scalar", "gpsimd", "sync")
ALL_PHASES = ("p0", "ln", "p1b", "p1q", "p1k", "p1v", "scan", "epi", "ffn")

SCRATCH_FLOW = {
    "gup_b": ("p0", ["ffn"]), "dwn_b": ("p0", ["ffn"]), "shg_b": ("p0", ["ffn"]),
    "shu_b": ("p0", ["ffn"]), "shd_b": ("p0", ["ffn"]),
    "Wq_b": ("p0", ["p1q"]), "Wk_b": ("p0", ["p1k"]), "Wv_b": ("p0", ["p1v"]),
    "Wo_b": ("p0", ["epi"]), "Wad_b": ("p0", ["p1b"]), "Wau_b": ("p0", ["p1b"]),
    "Wgd_b": ("p0", ["p1b"]), "Wgu_b": ("p0", ["p1b"]), "Wb_b": ("p0", ["p1b"]),
    "rw_b": ("p0", ["ffn"]), "sgw_b": ("p0", ["ffn"]),
    "xnf": ("ln", ["p1b", "p1q", "p1k", "p1v", "ffn"]),
    "eb_s": ("p1b", ["p1q", "p1k"]), "enb_s": ("p1b", ["p1k"]),
    "ebc_s": ("p1b", ["scan"]), "betaf_s": ("p1b", ["p1k", "p1v"]),
    "gate_s": ("p1b", ["epi"]),
    "qt_s": ("p1q", ["scan"]), "bkt_s": ("p1k", ["scan"]),
    "kb_s": ("p1k", ["scan"]), "vb_s": ("p1v", ["scan"]),
    "o_s": ("scan", ["epi"]), "og_s": ("epi", ["epi"]),
}


class Ctx:
    def __init__(self, nc, stack):
        self.nc = nc
        self.prog = {e: [] for e in ENGINES}
        self.sems = {}
        self.cnt = {}
        self.cur_phase = "init"
        self.phase_idx = 0
        self.enforced = {e: {} for e in ENGINES}
        for s in ["bar", "Ls", "Lg", "Lc", "St", "Sc", "T", "V", "A", "P", "G",
                  "XA", "XB", "EA", "EB", "BL", "SA", "SB", "TA", "TB",
                  "OA", "XL", "TL", "WL", "WQ", "S0", "S1", "S2", "S3",
                  "E2", "E3", "W0", "W1", "W2"]:
            self.sems[s] = stack.enter_context(nc.semaphore(s))
            self.cnt[s] = 0

    def emit(self, eng, fn):
        self.prog[eng].append((self.cur_phase, fn))

    def flush(self, eng, entries):
        i = 0
        while i < len(entries):
            ph = entries[i][0]
            j = i
            while j < len(entries) and entries[j][0] == ph:
                j += 1
            with self.nc.named_scope(ph):
                for _, fn in entries[i:j]:
                    fn(eng)
            i = j

    def inc(self, name, amt):
        self.cnt[name] += amt
        return (name, self.cnt[name])

    def all_of(self, sem):
        return (sem, self.cnt[sem])

    def wait(self, eng_name, deps):
        if deps is None:
            return
        if isinstance(deps, tuple):
            deps = [deps]
        best = {}
        for d in deps:
            if d is None:
                continue
            sn, v = d
            if v > best.get(sn, -1):
                best[sn] = v
        todo = []
        for sn, v in best.items():
            if self.enforced[eng_name].get(sn, -1) >= v:
                continue
            self.enforced[eng_name][sn] = v
            todo.append((self.sems[sn], v))
        if todo:
            def fn(eng, todo=todo):
                for s, v in todo:
                    eng.wait_ge(s, v)
            self.emit(eng_name, fn)

    def op(self, eng_name, fn, deps=None, inc=None):
        self.wait(eng_name, deps)
        if inc is None:
            self.emit(eng_name, fn)
            return None
        mark = self.inc(inc, 1)
        s = self.sems[inc]

        def fn2(eng, fn=fn, s=s):
            fn(eng).then_inc(s, 1)
        self.emit(eng_name, fn2)
        return mark

    def dma(self, queue, outap, inap, deps=None, sem=None, transpose=False,
            accum=False):
        if sem is None:
            sem = {"sync": "Ls", "scalar": "Sc", "gpsimd": "Lg"}[queue]
        self.wait(queue, deps)
        mark = self.inc(sem, 16)
        s = self.sems[sem]

        def fn(eng, outap=outap, inap=inap, s=s, transpose=transpose, accum=accum):
            kw = {}
            if transpose:
                kw["transpose"] = True
            if accum:
                kw["accum_op"] = Op.add
            eng.dma_start(out=outap, in_=inap, **kw).then_inc(s, 16)
        self.emit(queue, fn)
        return mark

    def barrier(self):
        self.phase_idx += 1
        tgt = 5 * self.phase_idx
        sem = self.sems["bar"]
        self.cnt["bar"] = tgt
        dma_sems = [(self.sems[sn], self.cnt[sn]) for sn in self.sems
                    if sn not in ("bar", "T", "V", "A", "P", "Lg")
                    and self.cnt[sn] > 0]
        own = {"tensor": "T", "vector": "V", "scalar": "A", "gpsimd": "P"}
        for e in ENGINES:
            waits = list(dma_sems)
            if e in own and self.cnt[own[e]] > 0:
                waits.append((self.sems[own[e]], self.cnt[own[e]]))
            def fn(eng, sem=sem, tgt=tgt, waits=waits):
                for s, v in waits:
                    eng.wait_ge(s, v)
                eng.sem_inc(sem, 1)
                eng.wait_ge(sem, tgt)
            self.emit(e, fn)
            for sn in self.sems:
                if sn != "Lg":
                    self.enforced[e][sn] = self.cnt[sn]


class Carver:
    def __init__(self, tensor, limit):
        self.t, self.limit, self.off = tensor, limit, 0

    def take(self, *shape):
        n = int(np.prod(shape))
        assert self.off + n <= self.limit, f"arena overflow {self.off}+{n}>{self.limit}"
        ap = self.t[:, self.off:self.off + n]
        self.off += n
        if len(shape) > 1:
            names = " ".join(f"d{i}" for i in range(len(shape)))
            ap = ap.rearrange(f"p ({names}) -> p {names}",
                              **{f"d{i}": s for i, s in enumerate(shape)})
        return ap

    def reset(self):
        self.off = 0


def fm_tile(dram2d, t0, tn):
    return dram2d[:, t0:t0 + tn].rearrange("(c p) t -> p c t", p=128)


def build_program(phases=ALL_PHASES, dbg=()):
    import contextlib
    nc = bass.Bass("TRN2", target_bir_lowering=False)
    stack = contextlib.ExitStack()

    W = {}
    for n, shp in [("Wq", [D, D]), ("Wk", [D, D]), ("Wv", [D, D]), ("Wo", [D, D]),
                   ("Wad", [D, DK]), ("Wau", [DK, D]), ("Wb", [D, NH]),
                   ("Wgd", [D, DV]), ("Wgu", [DV, D]),
                   ("cqw", [D, 4]), ("ckw", [D, 4]), ("cvw", [D, 4]),
                   ("rmsw", [DV]), ("rw", [E, D]), ("gup", [E, 2 * IMOE, D]),
                   ("dwn", [E, D, IMOE]), ("shg", [ISH, D]), ("shu", [ISH, D]),
                   ("shd", [D, ISH]), ("sgw", [1, D])]:
        W[n] = nc.dram_tensor(n, shp, F32, kind="ExternalInput").ap()
    x_ext = nc.dram_tensor("x_ext", [TEXT + 3, D], F32, kind="ExternalInput").ap()
    identb = nc.dram_tensor("identb", [128, 128], BF16, kind="ExternalInput").ap()
    maskN = nc.dram_tensor("maskN", [128, 512], F32, kind="ExternalInput").ap()
    maskU = nc.dram_tensor("maskU", [128, 512], F32, kind="ExternalInput").ap()
    out = nc.dram_tensor("out", [TOWN, D], F32, kind="ExternalOutput").ap()

    S = {}
    shapes = {
        "gup_b": ([E, 2 * IMOE, D], BF16), "dwn_b": ([E, D, IMOE], BF16),
        "shg_b": ([ISH, D], BF16), "shu_b": ([ISH, D], BF16),
        "shd_b": ([D, ISH], BF16),
        "Wq_b": ([D, D], BF16), "Wk_b": ([D, D], BF16), "Wv_b": ([D, D], BF16),
        "Wo_b": ([D, D], BF16), "Wad_b": ([D, DK], BF16),
        "Wau_b": ([DK, D], BF16), "Wgd_b": ([D, DV], BF16),
        "Wgu_b": ([DV, D], BF16), "Wb_b": ([D, NH], BF16),
        "rw_b": ([E, D], BF16), "sgw_b": ([1, D], BF16),
        "xnf": ([D, TEXT], BF16), "eb_s": ([D, TEXT], BF16),
        "enb_s": ([D, TEXT], BF16), "ebc_s": ([D, NCHK], F32),
        "betaf_s": ([NH, TEXT], BF16), "gate_s": ([TEXT, D], BF16),
        "qt_s": ([D, TEXT], BF16), "bkt_s": ([D, TEXT], BF16),
        "kb_s": ([D, TEXT], BF16), "vb_s": ([D, TEXT], BF16),
        "o_s": ([TEXT, D], BF16), "og_s": ([TOWN, D], BF16),
    }
    for name, (shp, dt) in shapes.items():
        prod, cons = SCRATCH_FLOW[name]
        has_p = prod in phases
        has_c = any(c in phases for c in cons)
        if name in dbg or (has_p and not has_c):
            kind = "ExternalOutput"
        elif has_c and not has_p:
            kind = "ExternalInput"
        elif has_p or has_c:
            kind = "Internal"
        else:
            continue
        S[name] = nc.dram_tensor(name, shp, dt, kind=kind).ap()

    AB = stack.enter_context(nc.sbuf_tensor("AB", [128, 66750], BF16))
    AFT = stack.enter_context(nc.sbuf_tensor("AFt", [128, 9000], F32))
    ident_sb = stack.enter_context(nc.sbuf_tensor("ident_sb", [128, 128], BF16))
    maskN_sb = stack.enter_context(nc.sbuf_tensor("maskN_sb", [128, 512], F32))
    maskU_sb = stack.enter_context(nc.sbuf_tensor("maskU_sb", [128, 512], F32))
    cw_sb = stack.enter_context(nc.sbuf_tensor("cw_sb", [128, 3, 16, 4], F32))
    comb_sb = stack.enter_context(nc.sbuf_tensor("comb_sb", [128, 8, 4, 8], F32))
    sg2_sb = stack.enter_context(nc.sbuf_tensor("sg2_sb", [128, 8, 4], F32))
    rmsbc_sb = stack.enter_context(nc.sbuf_tensor("rmsbc_sb", [128, 128], F32))
    onesb_sb = stack.enter_context(nc.sbuf_tensor("onesb_sb", [128, 1], BF16))
    ones1_sb = stack.enter_context(nc.sbuf_tensor("ones1_sb", [1, 128], F32))
    ones1b_sb = stack.enter_context(nc.sbuf_tensor("ones1b_sb", [1, 128], BF16))
    brow_sb = stack.enter_context(nc.sbuf_tensor("brow_sb", [1, 2, NH, 512], BF16))

    PF = [stack.enter_context(nc.psum_tensor(f"pf{i}", [128, 512], F32))
          for i in range(6)]
    PBF = [stack.enter_context(nc.psum_tensor(f"pbb{i}", [128, 1024], BF16))
           for i in range(2)]

    ctx = Ctx(nc, stack)
    fb = Carver(AFT, 9000)
    bb = Carver(AB, 66750)
    ns = dict(W=W, S=S, x_ext=x_ext, out=out, ident_sb=ident_sb,
              maskN_sb=maskN_sb, maskU_sb=maskU_sb, cw_sb=cw_sb,
              comb_sb=comb_sb, sg2_sb=sg2_sb, rmsbc_sb=rmsbc_sb,
              onesb_sb=onesb_sb, ones1_sb=ones1_sb, ones1b_sb=ones1b_sb,
              brow_sb=brow_sb,
              PF=PF, PBF=PBF, fb=fb, bb=bb)

    ctx.dma("sync", ident_sb[:, :], identb)
    ctx.dma("sync", maskN_sb[:, :], maskN)
    ctx.dma("sync", maskU_sb[:, :], maskU)
    ctx.dma("sync", cw_sb[:, 0], W["cqw"].rearrange("(c p) k -> p c k", p=128))
    ctx.dma("sync", cw_sb[:, 1], W["ckw"].rearrange("(c p) k -> p c k", p=128))
    cm = ctx.dma("sync", cw_sb[:, 2], W["cvw"].rearrange("(c p) k -> p c k", p=128))
    eps_sb = stack.enter_context(nc.sbuf_tensor("eps_sb", [128, 1], F32))
    nc.const_aps.register(F32, 1e-6, eps_sb[:, :])
    ctx.op("vector", lambda e: e.memset(eps_sb[:, :], 1e-6))
    ctx.op("vector", lambda e: e.memset(onesb_sb[:, :], 1.0))
    ctx.op("vector", lambda e: e.memset(ones1b_sb[:, :], 1.0))
    m1 = ctx.op("vector", lambda e: e.memset(ones1_sb[:, :], 1.0), inc="V")
    ctx.wait("tensor", [cm, m1])
    ctx.barrier()

    emitters = {"p0": emit_p0, "ln": emit_ln, "p1b": emit_p1b,
                "p1q": lambda c, n: _qkv_pass(c, n, "q"),
                "p1k": lambda c, n: _qkv_pass(c, n, "k"),
                "p1v": lambda c, n: _qkv_pass(c, n, "v"),
                "scan": emit_scan, "epi": emit_epi, "ffn": emit_ffn}
    for ph in ALL_PHASES:
        if ph not in phases:
            continue
        fb.reset(); bb.reset()
        ctx.cur_phase = ph
        emitters[ph](ctx, ns)
        ctx.barrier()
    ctx.cur_phase = "fin"

    for q in ("sync", "scalar", "gpsimd"):
        ctx.wait(q, [("St", ctx.cnt["St"]), ("Sc", ctx.cnt["Sc"]),
                     ("G", ctx.cnt["G"]), ("Lg", ctx.cnt["Lg"]),
                     ("S0", ctx.cnt["S0"]), ("S1", ctx.cnt["S1"]),
                     ("S2", ctx.cnt["S2"]), ("S3", ctx.cnt["S3"])])

    with nc.Block() as block:
        @block.tensor
        def _(eng):
            ctx.flush(eng, ctx.prog["tensor"])

        @block.vector
        def _(eng):
            ctx.flush(eng, ctx.prog["vector"])

        @block.scalar
        def _(eng):
            ctx.flush(eng, ctx.prog["scalar"])

        @block.gpsimd
        def _(eng):
            ctx.flush(eng, ctx.prog["gpsimd"])

        @block.sync
        def _(eng):
            ctx.flush(eng, ctx.prog["sync"])

    return nc, stack


def emit_p0(ctx, ns):
    W, S = ns["W"], ns["S"]
    for n in ("Wq", "Wk", "Wv", "Wo", "Wad", "Wau", "Wgd", "Wgu", "Wb",
              "rw", "sgw"):
        ctx.dma("gpsimd", S[n + "_b"], W[n], sem="Lc")
    for e_i in range(E):
        ctx.dma("gpsimd", S["gup_b"][e_i], W["gup"][e_i])
        ctx.dma("gpsimd", S["dwn_b"][e_i], W["dwn"][e_i])
    ctx.dma("gpsimd", S["shg_b"], W["shg"])
    ctx.dma("gpsimd", S["shu_b"], W["shu"])
    ctx.dma("gpsimd", S["shd_b"], W["shd"])


def emit_ln(ctx, ns):
    S, x_ext = ns["S"], ns["x_ext"]
    fb, bb, PBF = ns["fb"], ns["bb"], ns["PBF"]
    ident_sb = ns["ident_sb"]
    xin = fb.take(1, 4, 2048)
    sumc = fb.take(4, 4)
    ssq = fb.take(4, 1)
    rstd = fb.take(4, 1)
    xnt = bb.take(4, 2048)
    xnfst = bb.take(16, 512)

    loads = [ctx.dma("sync", xin[:, 0],
                     x_ext[3:3 + 512, :].rearrange("(g p) d -> p g d", p=128))]
    bank_guard = [None, None]
    xnt_guard = {g: [None, None] for g in range(4)}
    for ti in range(NTL):
        for g in range(4):
            xg = xin[:, 0, g]
            dep = loads[ti] if g == 0 else None
            m0 = ctx.op("vector", lambda e, xg=xg, g=g:
                        e.reduce_sum(sumc[:, g, 0:1], xg, axis=AX.X),
                        deps=dep, inc="V")
            ma = ctx.op("scalar", lambda e, xg=xg, g=g:
                        e.activation(xnt[:, g], xg, AF.Square,
                                     accum_out=ssq[:, g]),
                        deps=([dep] if dep else []) + list(xnt_guard[g]), inc="A")
            m1 = ctx.op("vector", lambda e, g=g:
                        e.tensor_single_scalar(sumc[:, g, 1:2], sumc[:, g, 0:1],
                                               1.0 / 2048, Op.mult),
                        deps=m0, inc="V")
            m2 = ctx.op("vector", lambda e, g=g:
                        e.tensor_tensor(sumc[:, g, 2:3], sumc[:, g, 1:2],
                                        sumc[:, g, 1:2], Op.mult),
                        deps=m1, inc="V")
            m3 = ctx.op("vector", lambda e, g=g:
                        e.tensor_scalar(sumc[:, g, 3:4], ssq[:, g],
                                        1.0 / 2048, sumc[:, g, 2:3],
                                        Op.mult, Op.subtract),
                        deps=[m2, ma], inc="V")
            m4 = ctx.op("vector", lambda e, g=g:
                        e.tensor_single_scalar(rstd[:, g], sumc[:, g, 3:4],
                                               1e-5, Op.add),
                        deps=m3, inc="V")
            mk2 = ctx.op("scalar", lambda e, g=g:
                         e.activation(rstd[:, g], rstd[:, g], AF.Sqrt),
                         deps=m4, inc="A")
            rq = ctx.op("vector", lambda e, g=g:
                        e.reciprocal(rstd[:, g], rstd[:, g]), deps=mk2, inc="V")
            mk3 = ctx.op("vector", lambda e, xg=xg, g=g:
                         e.tensor_scalar(xnt[:, g], xg, sumc[:, g, 1:2],
                                         rstd[:, g], Op.subtract, Op.mult),
                         deps=list(xnt_guard[g]) + [rq], inc="V")
            for half in range(2):
                pb = PBF[half]
                deps = [mk3, bank_guard[half]]
                tm = None
                for j in range(8):
                    kc = half * 8 + j
                    tm = ctx.op("tensor", lambda e, g=g, kc=kc, j=j, pb=pb:
                                e.transpose(pb[:, j * 128:(j + 1) * 128],
                                            xnt[:, g, kc * 128:(kc + 1) * 128],
                                            ident_sb[:, :]),
                                deps=deps if j == 0 else None,
                                inc="T" if j == 7 else None)
                bank_guard[half] = xnt_guard[g][half] = ctx.op(
                    "scalar", lambda e, g=g, half=half, pb=pb:
                    e.copy(xnfst[:, half * 8:half * 8 + 8,
                                 g * 128:(g + 1) * 128], pb[:, :]),
                    deps=[tm, ctx.all_of("St")] if (g == 0 and half == 0)
                    else tm, inc="A")
        ctx.dma("scalar", fm_tile(S["xnf"], ti * 512, 512), xnfst[:, :, :],
                deps=bank_guard[1], sem="St")
        if ti + 1 < NTL:
            loads.append(ctx.dma("sync", xin[:, 0],
                                 x_ext[3 + (ti + 1) * 512:3 + (ti + 2) * 512, :]
                                 .rearrange("(g p) d -> p g d", p=128),
                                 deps=[("V", ctx.cnt["V"])]))
        else:
            loads.append(None)


def xn_loader(ctx, ns, bufap, nbuf=2):
    S = ns["S"]

    def load(ti, deps=None):
        return ctx.dma("sync", bufap[:, ti % nbuf],
                       fm_tile(S["xnf"], ti * 512, 512), deps=deps,
                       sem="XA" if ti % 2 == 0 else "XB")
    return load


def emit_p1b(ctx, ns):
    W, S = ns["W"], ns["S"]
    fb, bb, PF, PBF = ns["fb"], ns["bb"], ns["PF"], ns["PBF"]
    ident_sb = ns["ident_sb"]

    xnfm = bb.take(2, 16, 512)
    wad_sb = bb.take(16, 128)
    wau_sb = bb.take(2048)
    wgd_sb = bb.take(16, 128)
    wgu_sb = bb.take(2048)
    wb_sb = bb.take(16, 16)
    g1s = bb.take(512)
    gt1s = bb.take(512)
    gateb = bb.take(2048)
    ebst = bb.take(512)
    betat = bb.take(64)
    gwk = fb.take(3, 512)
    cumf = fb.take(512)
    ebcf = fb.take(8)
    betar = bb.take(128)

    wl = [ctx.dma("scalar", wad_sb, S["Wad_b"].rearrange("(c p) n -> p c n", p=128), sem="WQ"),
          ctx.dma("scalar", wau_sb, S["Wau_b"], sem="WQ"),
          ctx.dma("scalar", wgd_sb, S["Wgd_b"].rearrange("(c p) n -> p c n", p=128), sem="WQ"),
          ctx.dma("scalar", wgu_sb, S["Wgu_b"], sem="WQ"),
          ctx.dma("scalar", wb_sb, S["Wb_b"].rearrange("(c p) n -> p c n", p=128), sem="WQ")]
    load = xn_loader(ctx, ns, xnfm)
    xw = [load(0), load(1)]
    guards = {i: None for i in range(8)}
    a3_guard = ac_guard = ebst_guard = ebcf_guard = betar_guard = None
    vc_prev = None

    for ti in range(NTL):
        p = ti % 2
        t0 = ti * 512
        deps = [xw[ti], guards[0]] + (wl if ti == 0 else [])
        g1m = None
        for kc in range(KC):
            g1m = ctx.op("tensor", lambda e, kc=kc, p=p:
                         e.matmul(PF[0][:, :], wad_sb[:, kc], xnfm[:, p, kc],
                                  start=(kc == 0), stop=(kc == 15)),
                         deps=deps if kc == 0 else None,
                         inc="T" if kc == 15 else None)
        guards[0] = ctx.op("vector", lambda e:
                           e.tensor_copy(g1s, PF[0][:, :]), deps=g1m, inc="V")
        g2m = None
        for kc in range(KC):
            g2m = ctx.op("tensor", lambda e, kc=kc, p=p:
                         e.matmul(PF[1][:, :], wgd_sb[:, kc], xnfm[:, p, kc],
                                  start=(kc == 0), stop=(kc == 15)),
                         deps=[guards[1]] if kc == 0 else None,
                         inc="T" if kc == 15 else None)
        guards[1] = ctx.op("vector", lambda e:
                           e.tensor_copy(gt1s, PF[1][:, :]), deps=g2m, inc="V")

        bm = None
        for g in range(4):
            for kc in range(KC):
                bm = ctx.op("tensor", lambda e, g=g, kc=kc, p=p:
                            e.matmul(PF[2][:, g * 16:(g + 1) * 16],
                                     xnfm[:, p, kc, g * 128:(g + 1) * 128],
                                     wb_sb[:, kc],
                                     start=(kc == 0), stop=(kc == 15)),
                            deps=[guards[2]] if (g == 0 and kc == 0) else None,
                            inc="T" if (g == 3 and kc == 15) else None)
        bclip = ctx.op("vector", lambda e:
                       e.tensor_scalar(betat, PF[2][:, 0:64], -10.0, 10.0,
                                       Op.max, Op.min), deps=bm, inc="V")
        guards[2] = bclip
        bsig = ctx.op("scalar", lambda e:
                      e.activation(betat, betat, AF.Sigmoid), deps=bclip, inc="A")
        bt = ctx.op("tensor", lambda e:
                    e.transpose(PBF[1][0:64, 0:128], betat, ident_sb[:, :]),
                    deps=[bsig, guards[7]], inc="T")
        guards[7] = ctx.op("vector", lambda e:
                           e.tensor_copy(betar[0:64, :], PBF[1][0:64, 0:128]),
                           deps=[bt, ctx.all_of("St")], inc="V")
        for g in range(4):
            betar_guard = ctx.dma(
                "scalar", S["betaf_s"][:, t0 + g * 128:t0 + (g + 1) * 128],
                betar[g * 16:(g + 1) * 16, :],
                deps=guards[7] if g == 0 else None, sem="St")

        for cc in range(KC):
            bk = 4 + (cc % 2)
            bank = PF[bk]
            g2 = ctx.op("tensor", lambda e, cc=cc, bank=bank:
                        e.matmul(bank[:, :], wau_sb[:, cc * 128:(cc + 1) * 128],
                                 g1s, start=True, stop=True),
                        deps=[guards[0], guards[bk]], inc="T")
            guards[bk] = vb = ctx.op("vector", lambda e, bank=bank:
                                     e.tensor_scalar(gwk[:, 0], bank[:, :],
                                                     -10.0, 10.0, Op.max, Op.min),
                                     deps=[g2, a3_guard, vc_prev], inc="V")
            ab1 = ctx.op("scalar", lambda e:
                         e.activation(gwk[:, 1], gwk[:, 0], AF.Abs), deps=vb,
                         inc="A")
            ab2 = ctx.op("scalar", lambda e:
                         e.activation(gwk[:, 1], gwk[:, 1], AF.Exp, scale=-1.0),
                         deps=ab1, inc="A")
            a2 = ctx.op("scalar", lambda e:
                        e.activation(gwk[:, 1], gwk[:, 1], AF.Ln, bias=1.0),
                        deps=ab2, inc="A")
            a3 = ctx.op("scalar", lambda e:
                        e.activation(gwk[:, 2], gwk[:, 0], AF.Relu), inc="A")
            a3_guard = a3
            vg = ctx.op("vector", lambda e:
                        e.scalar_tensor_tensor(gwk[:, 0], gwk[:, 2], -1.0,
                                               gwk[:, 1], Op.mult, Op.subtract),
                        deps=[a2, a3, vc_prev], inc="V")
            vc = None
            for sg8 in range(8):
                vc = ctx.op("vector", lambda e, sg8=sg8:
                            e.tensor_tensor_scan(cumf[:, sg8 * 64:(sg8 + 1) * 64],
                                                 gwk[:, 0, sg8 * 64:(sg8 + 1) * 64],
                                                 gwk[:, 0, sg8 * 64:(sg8 + 1) * 64],
                                                 0.0, Op.add, Op.bypass),
                            deps=[vg, ac_guard] if sg8 == 0 else None,
                            inc="V" if sg8 == 7 else None)
            vc_prev = vc
            ae = ctx.op("scalar", lambda e:
                        e.activation(ebst, cumf, AF.Exp),
                        deps=[vc, ctx.all_of("St")], inc="A")
            d1 = ctx.dma("sync", S["eb_s"][cc * 128:(cc + 1) * 128, t0:t0 + 512],
                         ebst, deps=ae, sem="St")
            ae2 = ctx.op("scalar", lambda e:
                         e.activation(ebst, cumf, AF.Exp, scale=-1.0),
                         deps=d1, inc="A")
            d2 = ctx.dma("sync", S["enb_s"][cc * 128:(cc + 1) * 128, t0:t0 + 512],
                         ebst, deps=ae2, sem="St")
            ebst_guard = None
            ac = ctx.op("scalar", lambda e:
                        e.activation(ebcf,
                                     cumf.rearrange("p (a b) -> p a b", b=64)[:, :, 63],
                                     AF.Exp), deps=ctx.all_of("St"), inc="A")
            ac_guard = ac
            ebcf_guard = ctx.dma("sync",
                                 S["ebc_s"][cc * 128:(cc + 1) * 128,
                                            ti * 8:ti * 8 + 8],
                                 ebcf, deps=ac, sem="St")
            guards[bk] = ctx.all_of("St")

        for g in range(4):
            glast = None
            for hc in range(4):
                bk = 4 + (hc % 2)
                bank = PF[bk]
                gm = ctx.op("tensor", lambda e, g=g, hc=hc, bank=bank:
                            e.matmul(bank[:, :], gt1s[:, g * 128:(g + 1) * 128],
                                     wgu_sb[:, hc * 512:(hc + 1) * 512],
                                     start=True, stop=True),
                            deps=[guards[1], guards[bk]], inc="T")
                guards[bk] = glast = ctx.op(
                    "scalar", lambda e, hc=hc, bank=bank:
                    e.activation(gateb[:, hc * 512:(hc + 1) * 512], bank[:, :],
                                 AF.Sigmoid), deps=gm, inc="A")
            gd = ctx.dma("sync", S["gate_s"][t0 + g * 128:t0 + (g + 1) * 128, :],
                         gateb, deps=glast, sem="St")
            guards[4] = guards[5] = ctx.all_of("St")
        if ti + 2 < NTL:
            xw.append(load(ti + 2, deps=[("T", ctx.cnt["T"])]))
        else:
            xw.append(None)


def _qkv_pass(ctx, ns, which):
    W, S = ns["W"], ns["S"]
    fb, bb, PF, PBF = ns["fb"], ns["bb"], ns["PF"], ns["PBF"]
    cw_sb, onesb_sb, ones1_sb, brow_sb = (ns["cw_sb"], ns["onesb_sb"],
                                          ns["ones1_sb"], ns["brow_sb"])
    ones1b_sb = ns["ones1b_sb"]
    ident_sb, x_ext = ns["ident_sb"], ns["x_ext"]

    wsb = bb.take(16, 2048)
    xnfm = bb.take(2, 16, 512)
    ebt = bb.take(2, 512)
    ebt2 = bb.take(2, 512)
    sqb = bb.take(2, 512)
    stg = bb.take(2, 512)
    stg2 = bb.take(2, 512)
    xn3b = bb.take(2048)
    xn3f = bb.take(16, 3)
    qs = fb.take(2, 515)
    conva = fb.take(2, 512)
    silf = fb.take(2, 512)
    knf = fb.take(2, 512)
    rrow = fb.take(2, 512)
    bndry = fb.take(16, 3)
    x3 = fb.take(2048)
    st3 = fb.take(4)
    mv3 = fb.take(4)
    pre_f = fb.take(16, 3)

    wn = {"q": "Wq", "k": "Wk", "v": "Wv"}[which]
    cwi = {"q": 0, "k": 1, "v": 2}[which]
    wld = ctx.dma("scalar", wsb, S[wn + "_b"].rearrange("(c p) n -> p c n", p=128), sem="WQ")
    load = xn_loader(ctx, ns, xnfm)
    xw = [load(0), load(1)]

    l3 = ctx.dma("sync", x3[0:3, :], x_ext[0:3, :])
    p0m = ctx.op("vector", lambda e:
                 e.reduce_sum(st3[0:3, 0:1], x3[0:3, :], axis=AX.X),
                 deps=l3, inc="V")
    p0a = ctx.op("scalar", lambda e:
                 e.activation(xn3b[0:3, :], x3[0:3, :], AF.Square,
                              accum_out=st3[0:3, 2:3]), deps=l3, inc="A")
    p1m = ctx.op("vector", lambda e:
                 e.tensor_single_scalar(mv3[0:3, 0:1], st3[0:3, 0:1],
                                        1.0 / 2048, Op.mult), deps=p0m, inc="V")
    p2m = ctx.op("vector", lambda e:
                 e.tensor_tensor(mv3[0:3, 2:3], mv3[0:3, 0:1], mv3[0:3, 0:1],
                                 Op.mult), deps=p1m, inc="V")
    p3m = ctx.op("vector", lambda e:
                 e.tensor_scalar(mv3[0:3, 1:2], st3[0:3, 2:3], 1.0 / 2048,
                                 mv3[0:3, 2:3], Op.mult, Op.subtract),
                 deps=[p2m, p0a], inc="V")
    p4m = ctx.op("vector", lambda e:
                 e.tensor_single_scalar(mv3[0:3, 1:2], mv3[0:3, 1:2],
                                        1e-5, Op.add), deps=p3m, inc="V")
    a3m = ctx.op("scalar", lambda e:
                 e.activation(mv3[0:3, 1:2], mv3[0:3, 1:2], AF.Sqrt),
                 deps=p4m, inc="A")
    p5m = ctx.op("vector", lambda e:
                 e.reciprocal(mv3[0:3, 1:2], mv3[0:3, 1:2]),
                 deps=a3m, inc="V")
    xn3m = ctx.op("vector", lambda e:
                  e.tensor_scalar(xn3b[0:3, :], x3[0:3, :], mv3[0:3, 0:1],
                                  mv3[0:3, 1:2], Op.subtract, Op.mult),
                  deps=p5m, inc="V")
    t3 = None
    for kc in range(KC):
        t3 = ctx.op("tensor", lambda e, kc=kc:
                    e.transpose(PBF[0][:, kc * 4:kc * 4 + 3],
                                xn3b[0:3, kc * 128:(kc + 1) * 128],
                                ident_sb[0:3, 0:3]),
                    deps=xn3m if kc == 0 else None,
                    inc="T" if kc == 15 else None)
    ev3 = ctx.op("scalar", lambda e:
                 e.copy(xn3f,
                        PBF[0][:, 0:64].rearrange("p (a b) -> p a b", b=4)[:, :, 0:3]),
                 deps=t3, inc="A")
    pm = None
    for cc in range(KC):
        for kc in range(KC):
            pm = ctx.op("tensor", lambda e, cc=cc, kc=kc:
                        e.matmul(PF[3][:, cc * 3:(cc + 1) * 3],
                                 wsb[:, kc, cc * 128:(cc + 1) * 128],
                                 xn3f[:, kc],
                                 start=(kc == 0), stop=(kc == 15)),
                        deps=[ev3, wld] if (cc == 0 and kc == 0) else None,
                        inc="T" if (cc == 15 and kc == 15) else None)
    pv = ctx.op("vector", lambda e:
                e.tensor_copy(pre_f.rearrange("p a b -> p (a b)"), PF[3][:, 0:48]),
                deps=pm, inc="V")
    binit = ctx.op("vector", lambda e:
                   e.tensor_copy(bndry.rearrange("p a b -> p (a b)"),
                                 pre_f.rearrange("p a b -> p (a b)")),
                   deps=pv, inc="V")

    guards = {0: None, 1: None}
    bc_g = [pv, None]
    sm_g = [None, None]
    sq_g = [None, None]
    qs_g = [None, None]
    silf_g = [None, None]
    knf_g = [None, None]
    rr_g = [None, None]
    st_g = [None, None]
    st2_g = [None, None]
    conva_g = [None, None]
    bc2_g = None
    prev_bnd = {cc: binit for cc in range(KC)}
    el_guard = [None, None]
    el2_guard = [None, None]
    needs_beta = which in ("k", "v")
    bload = []
    if needs_beta:
        bload.append(ctx.dma("sync", brow_sb[:, 0],
                             S["betaf_s"][:, 0:512].rearrange("(o h) t -> o h t", o=1),
                             sem="BL"))

    for ti in range(NTL):
        p = ti % 2
        bp = ti % 2
        t0 = ti * 512
        for cc in range(KC):
            gk = cc % 2
            bank = PF[gk]
            deps = [xw[ti], guards[gk]] + ([wld] if (ti == 0 and cc == 0) else [])
            mm = None
            for kc in range(KC):
                mm = ctx.op("tensor", lambda e, cc=cc, kc=kc, p=p, bank=bank:
                            e.matmul(bank[:, :],
                                     wsb[:, kc, cc * 128:(cc + 1) * 128],
                                     xnfm[:, p, kc],
                                     start=(kc == 0), stop=(kc == 15)),
                            deps=deps if kc == 0 else None,
                            inc="T" if kc == 15 else None)
            cb = ctx.op("scalar", lambda e, cc=cc, gk=gk:
                        e.copy(qs[:, gk, 0:3], bndry[:, cc]),
                        deps=[prev_bnd[cc], qs_g[gk]], inc="A")
            ce = ctx.op("scalar", lambda e, bank=bank, gk=gk:
                        e.copy(qs[:, gk, 3:515], bank[:, :]), deps=mm, inc="A")
            guards[gk] = ce
            prev_bnd[cc] = ctx.op("vector", lambda e, cc=cc, gk=gk:
                                  e.tensor_copy(bndry[:, cc], qs[:, gk, 512:515]),
                                  deps=ce, inc="V")
            cj = ctx.op("vector", lambda e, cc=cc, gk=gk:
                        e.tensor_scalar(conva[:, gk], qs[:, gk, 0:512],
                                        cw_sb[:, cwi, cc, 0:1],
                                        None, Op.mult, Op.bypass),
                        deps=[cb, ce, conva_g[gk]], inc="V")
            for j in (1, 2, 3):
                cj = ctx.op("vector", lambda e, cc=cc, j=j, gk=gk:
                            e.scalar_tensor_tensor(conva[:, gk], qs[:, gk, j:j + 512],
                                                   cw_sb[:, cwi, cc, j:j + 1],
                                                   conva[:, gk], Op.mult, Op.add),
                            deps=cj, inc="V")
            qs_g[gk] = cj
            sl = ctx.op("scalar", lambda e, gk=gk:
                        e.activation(silf[:, gk], conva[:, gk], AF.Silu),
                        deps=[cj, silf_g[gk]], inc="A")
            conva_g[gk] = sl

            if which in ("q", "k"):
                sq = ctx.op("scalar", lambda e, gk=gk:
                            e.activation(sqb[:, gk], silf[:, gk], AF.Square),
                            deps=[sl, sq_g[gk]], inc="A")
                smm = ctx.op("tensor", lambda e, gk=gk:
                             e.matmul(PF[2][32 * gk:32 * gk + 1, 0:512],
                                      onesb_sb[:, :],
                                      sqb[:, gk], start=True, stop=True),
                             deps=[sq, sm_g[gk]], inc="T")
                sq_g[gk] = smm
                sv = ctx.op("scalar", lambda e, gk=gk:
                            e.activation(rrow[0:1, gk],
                                         PF[2][32 * gk:32 * gk + 1, 0:512],
                                         AF.Sqrt), deps=[smm, rr_g[gk]], inc="A")
                sm_g[gk] = sv
                mxm = ctx.op("vector", lambda e, gk=gk:
                             e.tensor_single_scalar(rrow[0:1, gk], rrow[0:1, gk],
                                                    1e-6, Op.max),
                             deps=sv, inc="V")
                rv = ctx.op("vector", lambda e, gk=gk:
                            e.reciprocal(rrow[0:1, gk], rrow[0:1, gk]),
                            deps=mxm, inc="V")
                bc = ctx.op("tensor", lambda e, gk=gk:
                            e.matmul(PF[3 + gk][:, 0:512], ones1_sb[:, :],
                                     rrow[0:1, gk], start=True, stop=True),
                            deps=[rv, bc_g[gk]], inc="T")
                rr_g[gk] = bc
                if which == "q":
                    nm = ctx.op("vector", lambda e, gk=gk:
                                e.tensor_tensor(silf[:, gk], silf[:, gk],
                                                PF[3 + gk][:, 0:512],
                                                Op.mult), deps=bc, inc="V")
                    bc_g[gk] = nm
                    el = ctx.dma("sync", ebt[:, gk],
                                 S["eb_s"][cc * 128:(cc + 1) * 128, t0:t0 + 512],
                                 deps=el_guard[gk],
                                 sem="EA" if gk == 0 else "EB")
                    qt = ctx.op("vector", lambda e, cc=cc, gk=gk:
                                e.tensor_tensor(stg[:, gk], silf[:, gk],
                                                ebt[:, gk], Op.mult),
                                deps=[nm, el, st_g[gk]], inc="V")
                    el_guard[gk] = qt
                    silf_g[gk] = qt
                    st_g[gk] = ctx.dma(
                        "scalar", S["qt_s"][cc * 128:(cc + 1) * 128, t0:t0 + 512],
                        stg[:, gk], deps=qt, sem="S0" if gk == 0 else "S1")
                else:
                    nm = ctx.op("vector", lambda e, gk=gk:
                                e.tensor_tensor(knf[:, gk], silf[:, gk],
                                                PF[3 + gk][:, 0:512],
                                                Op.mult),
                                deps=[bc, knf_g[gk]], inc="V")
                    bc_g[gk] = nm
                    silf_g[gk] = nm
                    el2 = ctx.dma("sync", ebt2[:, gk],
                                  S["enb_s"][cc * 128:(cc + 1) * 128, t0:t0 + 512],
                                  deps=el2_guard[gk],
                                  sem="E2" if gk == 0 else "E3")
                    kb = ctx.op("vector", lambda e, cc=cc, gk=gk:
                                e.tensor_tensor(stg2[:, gk], knf[:, gk],
                                                ebt2[:, gk], Op.mult),
                                deps=[nm, el2, st2_g[gk]], inc="V")
                    el2_guard[gk] = kb
                    st2_g[gk] = ctx.dma(
                        "scalar", S["kb_s"][cc * 128:(cc + 1) * 128, t0:t0 + 512],
                        stg2[:, gk], deps=kb, sem="S2" if gk == 0 else "S3")
                    bc2 = ctx.op("tensor", lambda e, cc=cc, bp=bp:
                                 e.matmul(PF[5][:, 0:512], ones1b_sb[:, :],
                                          brow_sb[0:1, bp, cc], start=True,
                                          stop=True),
                                 deps=[bload[ti], bc2_g], inc="T")
                    nb = ctx.op("vector", lambda e, gk=gk:
                                e.tensor_tensor(knf[:, gk], knf[:, gk],
                                                PF[5][:, 0:512],
                                                Op.mult), deps=[bc2, kb], inc="V")
                    bc2_g = nb
                    el = ctx.dma("sync", ebt[:, gk],
                                 S["eb_s"][cc * 128:(cc + 1) * 128, t0:t0 + 512],
                                 deps=el_guard[gk],
                                 sem="EA" if gk == 0 else "EB")
                    kt = ctx.op("vector", lambda e, cc=cc, gk=gk:
                                e.tensor_tensor(stg[:, gk], knf[:, gk],
                                                ebt[:, gk], Op.mult),
                                deps=[nb, el, st_g[gk]], inc="V")
                    el_guard[gk] = kt
                    knf_g[gk] = kt
                    st_g[gk] = ctx.dma(
                        "scalar", S["bkt_s"][cc * 128:(cc + 1) * 128, t0:t0 + 512],
                        stg[:, gk], deps=kt, sem="S0" if gk == 0 else "S1")
            else:
                bc = ctx.op("tensor", lambda e, cc=cc, gk=gk, bp=bp:
                            e.matmul(PF[3 + gk][:, 0:512], ones1b_sb[:, :],
                                     brow_sb[0:1, bp, cc], start=True, stop=True),
                            deps=[bload[ti], bc_g[gk], sl], inc="T")
                nm = ctx.op("vector", lambda e, gk=gk:
                            e.tensor_tensor(stg[:, gk], silf[:, gk],
                                            PF[3 + gk][:, 0:512], Op.mult),
                            deps=[bc, st_g[gk]], inc="V")
                bc_g[gk] = nm
                silf_g[gk] = nm
                st_g[gk] = ctx.dma(
                    "scalar", S["vb_s"][cc * 128:(cc + 1) * 128, t0:t0 + 512],
                    stg[:, gk], deps=nm, sem="S0" if gk == 0 else "S1")
        if ti + 2 < NTL:
            xw.append(load(ti + 2, deps=[("T", ctx.cnt["T"])]))
        else:
            xw.append(None)
        if needs_beta:
            if ti + 1 < NTL:
                bload.append(ctx.dma("sync", brow_sb[:, (ti + 1) % 2],
                                     S["betaf_s"][:, (ti + 1) * 512:(ti + 2) * 512]
                                     .rearrange("(o h) t -> o h t", o=1),
                                     deps=[("T", ctx.cnt["T"])], sem="BL"))
            else:
                bload.append(None)


SCAN_STOP = None
FFN_PLAIN_W = False
FFN_NO_ACCUM = False
SCAN_NCHK = None

def emit_scan(ctx, ns):
    S = ns["S"]
    fb, bb, PF, PBF = ns["fb"], ns["bb"], ns["PF"], ns["PBF"]
    ident_sb, maskN_sb, maskU_sb = ns["ident_sb"], ns["maskN_sb"], ns["maskU_sb"]

    sp_qt = bb.take(2, 16, 128)
    sp_bkt = bb.take(2, 16, 128)
    sp_kb = bb.take(2, 16, 128)
    sp_vt = bb.take(2, 2048)
    sp_kbt = bb.take(2, 2048)
    kbt_b = bb.take(2048)
    Nt = bb.take(8, 128)
    NTt = bb.take(8, 128)
    Xa = bb.take(8, 128)
    Xb = bb.take(8, 128)
    XT_ = bb.take(5, 8, 128)
    XT = [XT_[:, i] for i in range(5)]
    r2 = bb.take(2, 8, 128)
    u2 = bb.take(8, 128)
    attnT = bb.take(8, 128)
    obuf = bb.take(2048)
    S_b = bb.take(16, 128)
    S_f = fb.take(16, 128)
    Stmp = fb.take(128)
    ebc_t = fb.take(2, 16, 2)

    z1 = ctx.op("vector", lambda e:
                e.memset(S_f.rearrange("p a b -> p (a b)"), 0.0), inc="V")
    z2 = ctx.op("vector", lambda e:
                e.memset(S_b.rearrange("p a b -> p (a b)"), 0.0), inc="V")
    zp0 = ctx.op("vector", lambda e: e.memset(PF[0][:, :], 0.0), inc="V")
    zp1 = ctx.op("vector", lambda e: e.memset(PF[1][:, :], 0.0), inc="V")

    nchk = SCAN_NCHK or NCHK
    NSPAN = nchk // 2

    def span_load(sp, deps=None):
        t0 = sp * 128
        pbuf = sp % 2
        sl = "SA" if pbuf == 0 else "SB"
        tl = "TA" if pbuf == 0 else "TB"
        m = []
        m.append(ctx.dma("sync", sp_qt[:, pbuf],
                         S["qt_s"][:, t0:t0 + 128]
                         .rearrange("(h p) t -> p h t", p=128), deps=deps, sem=sl))
        m.append(ctx.dma("sync", sp_bkt[:, pbuf],
                         S["bkt_s"][:, t0:t0 + 128]
                         .rearrange("(h p) t -> p h t", p=128), sem=sl))
        m.append(ctx.dma("sync", sp_kb[:, pbuf],
                         S["kb_s"][:, t0:t0 + 128]
                         .rearrange("(h p) t -> p h t", p=128), sem=sl))
        m.append(ctx.dma("scalar", sp_vt[:, pbuf], S["vb_s"][:, t0:t0 + 128],
                         transpose=True, deps=deps, sem=tl))
        m.append(ctx.dma("scalar", sp_kbt[:, pbuf], S["kb_s"][:, t0:t0 + 128],
                         transpose=True, sem=tl))
        m.append(ctx.dma("sync", ebc_t[:, pbuf],
                         S["ebc_s"][:, sp * 2:sp * 2 + 2]
                         .rearrange("(h p) c -> p h c", p=128), sem=sl))
        return m

    spl = [span_load(0, deps=[z1, z2])]
    gA = [zp0, zp1]
    gSQ = [None, None]
    gR = [None, None]
    gNT = None
    s_dep = {h: (z1 if True else None) for h in range(NH)}
    sf_dep = {h: z1 for h in range(NH)}
    prev_stt = z1

    for n in range(nchk):
        sp, cs = n // 2, n % 2
        pbuf = sp % 2
        if cs == 0 and sp + 1 < NSPAN:
            spl.append(span_load(sp + 1,
                                 deps=[("T", ctx.cnt["T"]), ("V", ctx.cnt["V"]),
                                       ("P", ctx.cnt["P"])]))
        ldep = spl[sp]
        qt = sp_qt[:, pbuf]
        bkt = sp_bkt[:, pbuf]
        kb = sp_kb[:, pbuf]
        vt = sp_vt[:, pbuf]
        kbt = sp_kbt[:, pbuf]
        kbtm = None
        if cs == 1:
            kbtm = ctx.op("scalar", lambda e, kbt=kbt:
                          e.copy(kbt_b[0:64, :], kbt[64:128, :]),
                          deps=ldep + [("T", ctx.cnt["T"])], inc="A")

        am = [None, None]
        for pi in range(NPAIR):
            bnk = pi // 4
            col = (pi % 4) * 128
            h0, h1 = 2 * pi, 2 * pi + 1
            dd = ldep + [gA[bnk]] if pi % 4 == 0 else None
            ctx.op("tensor", lambda e, bnk=bnk, col=col, h0=h0, cs=cs, bkt=bkt, kb=kb:
                   e.matmul(PF[bnk][0:64, col:col + 64],
                            bkt[:, h0, cs * 64:(cs + 1) * 64],
                            kb[:, h0, cs * 64:(cs + 1) * 64],
                            start=True, stop=True), deps=dd)
            am[bnk] = ctx.op("tensor", lambda e, bnk=bnk, col=col, h1=h1, cs=cs, bkt=bkt, kb=kb:
                             e.matmul(PF[bnk][64:128, col + 64:col + 128],
                                      bkt[:, h1, cs * 64:(cs + 1) * 64],
                                      kb[:, h1, cs * 64:(cs + 1) * 64],
                                      start=True, stop=True),
                             inc="T" if pi % 4 == 3 else None)
        if SCAN_STOP == 'A':
            continue
        nmk = [None, None]
        for bnk in range(2):
            nmk[bnk] = ctx.op("vector", lambda e, bnk=bnk:
                              e.tensor_tensor(Nt[:, 4 * bnk:4 * bnk + 4]
                                              .rearrange("p a b -> p (a b)"),
                                              PF[bnk][:, :], maskN_sb[:, :],
                                              Op.mult), deps=am[bnk], inc="V")
        gA = [nmk[0], nmk[1]]
        if SCAN_STOP == 'N':
            continue
        tt = None
        for pi in range(NPAIR):
            tt = ctx.op("tensor", lambda e, pi=pi:
                        e.transpose(PBF[0][:, pi * 128:(pi + 1) * 128],
                                    Nt[:, pi], ident_sb[:, :]),
                        deps=[nmk[pi // 4], gNT] if pi == 0 else
                        ([nmk[1]] if pi == 4 else None),
                        inc="T" if pi == 7 else None)
        gNT = ctx.op("scalar", lambda e:
                     e.copy(NTt.rearrange("p a b -> p (a b)"), PBF[0][:, :]),
                     deps=tt, inc="A")

        if SCAN_STOP == 'NT':
            continue
        sq_steps = [
            (NTt, Nt, Xa),
            (Nt, NTt, XT[0]),
            (XT[0], Xa, Xb),
            (Xa, XT[0], XT[1]),
            (XT[1], Xb, Xa),
            (Xb, XT[1], XT[2]),
            (XT[2], Xa, Xb),
            (Xa, XT[2], XT[3]),
            (Xb, XT[3], XT[4]),
        ]
        prev_ev = [gNT, gNT]
        for si, (lhs, rhs, dst) in enumerate(sq_steps):
            smm = [None, None]
            for pi in range(NPAIR):
                bnk = 2 + pi // 4
                col = (pi % 4) * 128
                dd = [prev_ev[0], prev_ev[1], gSQ[pi // 4]] if pi % 4 == 0 else None
                smm[pi // 4] = ctx.op(
                    "tensor", lambda e, bnk=bnk, col=col, pi=pi, lhs=lhs, rhs=rhs:
                    e.matmul(PF[bnk][:, col:col + 128], lhs[:, pi], rhs[:, pi],
                             start=True, stop=True),
                    deps=dd, inc="T" if pi % 4 == 3 else None)
            ev = [None, None]
            for bnk in range(2):
                if si % 2 == 0:
                    ev[bnk] = ctx.op("scalar", lambda e, bnk=bnk, dst=dst:
                                     e.copy(dst[:, 4 * bnk:4 * bnk + 4]
                                            .rearrange("p a b -> p (a b)"),
                                            PF[2 + bnk][:, :]),
                                     deps=smm[bnk], inc="A")
                else:
                    ev[bnk] = ctx.op("vector", lambda e, bnk=bnk, dst=dst:
                                     e.tensor_copy(dst[:, 4 * bnk:4 * bnk + 4]
                                                   .rearrange("p a b -> p (a b)"),
                                                   PF[2 + bnk][:, :]),
                                     deps=smm[bnk], inc="V")
            gSQ = ev
            prev_ev = ev

        if SCAN_STOP == 'SQ':
            continue
        km = [None, None]
        for pi in range(NPAIR):
            bnk = 4 + pi // 4
            col = (pi % 4) * 128
            h0, h1 = 2 * pi, 2 * pi + 1
            ctx.op("tensor", lambda e, bnk=bnk, col=col, h0=h0, cs=cs, bkt=bkt:
                   e.matmul(PF[bnk][0:64, col:col + 128],
                            bkt[:, h0, cs * 64:(cs + 1) * 64], S_b[:, h0],
                            start=True, stop=True),
                   deps=[gR[pi // 4], s_dep[h0], s_dep[h1]])
            km[pi // 4] = ctx.op(
                "tensor", lambda e, bnk=bnk, col=col, h1=h1, cs=cs, bkt=bkt:
                e.matmul(PF[bnk][64:128, col:col + 128],
                         bkt[:, h1, cs * 64:(cs + 1) * 64], S_b[:, h1],
                         start=True, stop=True),
                inc="T" if pi % 4 == 3 else None)
        if SCAN_STOP == 'KS':
            continue
        rv = [None, None]
        for pi in range(NPAIR):
            bnk = 4 + pi // 4
            col = (pi % 4) * 128
            h0, h1 = 2 * pi, 2 * pi + 1
            ctx.op("vector", lambda e, bnk=bnk, col=col, h0=h0, pi=pi, cs=cs, vt=vt:
                   e.tensor_tensor(r2[:, 0, pi][0:64, :],
                                   vt[cs * 64:cs * 64 + 64, h0 * 128:(h0 + 1) * 128],
                                   PF[bnk][0:64, col:col + 128], Op.subtract),
                   deps=km[pi // 4] if pi % 4 == 0 else None)
            rv[pi // 4] = ctx.op(
                "vector", lambda e, bnk=bnk, col=col, h1=h1, pi=pi, cs=cs, vt=vt:
                e.tensor_tensor(r2[:, 0, pi][64:128, :],
                                vt[cs * 64:cs * 64 + 64, h1 * 128:(h1 + 1) * 128],
                                PF[bnk][64:128, col:col + 128], Op.subtract),
                inc="V" if pi % 4 == 3 else None)
        gR = rv
        if SCAN_STOP == 'RHS':
            continue
        app_lhs = [XT[4], XT[3], XT[2], XT[1], XT[0], NTt]
        sqdone = list(prev_ev)
        rsrc, rdst = 0, 1
        prev_rv = rv
        for lvl in range(6):
            lmm = [None, None]
            for pi in range(NPAIR):
                bnk = 4 + pi // 4
                col = (pi % 4) * 128
                dd = ([prev_rv[0], prev_rv[1], gR[pi // 4]] + sqdone) \
                    if pi % 4 == 0 else None
                lmm[pi // 4] = ctx.op(
                    "tensor",
                    lambda e, bnk=bnk, col=col, pi=pi, lhs=app_lhs[lvl], rs=rsrc:
                    e.matmul(PF[bnk][:, col:col + 128], lhs[:, pi], r2[:, rs, pi],
                             start=True, stop=True),
                    deps=dd, inc="T" if pi % 4 == 3 else None)
            nev = [None, None]
            for bnk in range(2):
                nev[bnk] = ctx.op(
                    "vector", lambda e, bnk=bnk, rs=rsrc, rd=rdst:
                    e.tensor_tensor(r2[:, rd, 4 * bnk:4 * bnk + 4]
                                    .rearrange("p a b -> p (a b)"),
                                    PF[4 + bnk][:, :],
                                    r2[:, rs, 4 * bnk:4 * bnk + 4]
                                    .rearrange("p a b -> p (a b)"), Op.add),
                    deps=lmm[bnk], inc="V")
            gR = nev
            prev_rv = nev
            rsrc, rdst = rdst, rsrc
        u_idx = rsrc
        u_mark = prev_rv

        if SCAN_STOP == 'APP':
            continue
        atm = [None, None]
        for pi in range(NPAIR):
            bnk = pi // 4
            col = (pi % 4) * 128
            h0, h1 = 2 * pi, 2 * pi + 1
            dd = [gA[0], gA[1]] if pi == 0 else None
            ctx.op("tensor", lambda e, bnk=bnk, col=col, h0=h0, cs=cs, kb=kb, qt=qt:
                   e.matmul(PF[bnk][0:64, col:col + 64],
                            kb[:, h0, cs * 64:(cs + 1) * 64],
                            qt[:, h0, cs * 64:(cs + 1) * 64],
                            start=True, stop=True), deps=dd)
            atm[bnk] = ctx.op("tensor", lambda e, bnk=bnk, col=col, h1=h1, cs=cs, kb=kb, qt=qt:
                              e.matmul(PF[bnk][64:128, col + 64:col + 128],
                                       kb[:, h1, cs * 64:(cs + 1) * 64],
                                       qt[:, h1, cs * 64:(cs + 1) * 64],
                                       start=True, stop=True),
                              inc="T" if pi % 4 == 3 else None)
        amk = [None, None]
        for bnk in range(2):
            amk[bnk] = ctx.op("vector", lambda e, bnk=bnk:
                              e.tensor_tensor(attnT[:, 4 * bnk:4 * bnk + 4]
                                              .rearrange("p a b -> p (a b)"),
                                              PF[bnk][:, :], maskU_sb[:, :],
                                              Op.mult), deps=atm[bnk], inc="V")
        gA = amk

        if SCAN_STOP == 'ATT':
            continue
        om = [None, None]
        for pi in range(NPAIR):
            bnk = 2 + pi // 4
            col = (pi % 4) * 128
            h0, h1 = 2 * pi, 2 * pi + 1
            dd = [gSQ[0], gSQ[1], amk[pi // 4], u_mark[0], u_mark[1]] \
                if pi % 4 == 0 else None
            ctx.op("tensor", lambda e, bnk=bnk, col=col, h0=h0, cs=cs, qt=qt:
                   e.matmul(PF[bnk][0:64, col:col + 128],
                            qt[:, h0, cs * 64:(cs + 1) * 64], S_b[:, h0],
                            start=True, stop=False), deps=dd)
            ctx.op("tensor", lambda e, bnk=bnk, col=col, pi=pi, ui=u_idx:
                   e.matmul(PF[bnk][0:64, col:col + 128],
                            attnT[:, pi][:, 0:64], r2[:, ui, pi],
                            start=False, stop=True))
            ctx.op("tensor", lambda e, bnk=bnk, col=col, h1=h1, cs=cs, qt=qt:
                   e.matmul(PF[bnk][64:128, col:col + 128],
                            qt[:, h1, cs * 64:(cs + 1) * 64], S_b[:, h1],
                            start=True, stop=False))
            om[pi // 4] = ctx.op(
                "tensor", lambda e, bnk=bnk, col=col, pi=pi, ui=u_idx:
                e.matmul(PF[bnk][64:128, col:col + 128],
                         attnT[:, pi][:, 64:128], r2[:, ui, pi],
                         start=False, stop=True),
                inc="T" if pi % 4 == 3 else None)
        oev = None
        for pi in range(NPAIR):
            bnk = 2 + pi // 4
            col = (pi % 4) * 128
            h0, h1 = 2 * pi, 2 * pi + 1
            dd = [om[pi // 4]] if pi % 4 == 0 else None
            if pi == 0 and cs == 0:
                dd = [om[0], ctx.all_of("St")]
            ctx.op("scalar", lambda e, bnk=bnk, col=col, h0=h0, cs=cs:
                   e.copy(obuf[cs * 64:cs * 64 + 64, h0 * 128:(h0 + 1) * 128],
                          PF[bnk][0:64, col:col + 128]),
                   deps=dd)
            oev = ctx.op("scalar", lambda e, bnk=bnk, col=col, h1=h1, cs=cs:
                         e.copy(obuf[cs * 64:cs * 64 + 64,
                                     h1 * 128:(h1 + 1) * 128],
                                PF[bnk][64:128, col:col + 128]),
                         inc="A" if pi == 7 else None)
        gSQ = [oev, oev]

        if SCAN_STOP == 'O':
            continue
        for rnd in range(2):
            if rnd == 0:
                u2m = ctx.op("vector", lambda e, ui=u_idx:
                             e.tensor_copy(u2.rearrange("p a b -> p (a b)")[0:64, :],
                                           r2[:, ui].rearrange("p a b -> p (a b)")
                                           [64:128, :]),
                             deps=[u_mark[0], u_mark[1],
                                   ("T", ctx.cnt["T"])], inc="V")
            sam = [None, None]
            for hh in range(8):
                h = rnd * 8 + hh
                pi = h // 2
                bnk = 4 + hh // 4
                col = (hh % 4) * 128
                half = h % 2
                ksrc = kbt if cs == 0 else kbt_b
                dd = [gR[0], gR[1], kbtm, u2m] if hh == 0 else None
                sam[hh // 4] = ctx.op(
                    "tensor",
                    lambda e, bnk=bnk, col=col, h=h, pi=pi, half=half, cs=cs,
                    ui=u_idx, ksrc=ksrc:
                    e.matmul(PF[bnk][:, col:col + 128],
                             ksrc[0:64, h * 128:(h + 1) * 128],
                             (r2[:, ui, pi] if half == 0 else u2[:, pi])[0:64, :],
                             start=True, stop=True),
                    deps=dd, inc="T" if hh % 4 == 3 else None)
            upd = [None, None]
            for hh in range(8):
                h = rnd * 8 + hh
                bnk = 4 + hh // 4
                col = (hh % 4) * 128
                stm = ctx.op("vector",
                             lambda e, bnk=bnk, col=col, h=h, pbuf=pbuf, cs=cs:
                             e.tensor_scalar(Stmp, PF[bnk][:, col:col + 128],
                                             ebc_t[:, pbuf, h, cs:cs + 1], None,
                                             Op.mult, Op.bypass),
                             deps=([sam[hh // 4]] if hh % 4 == 0 else [])
                             + [sf_dep[h], prev_stt], inc="V")
                sf_dep[h] = prev_stt = ctx.op(
                    "vector", lambda e, h=h, pbuf=pbuf, cs=cs:
                    e.scalar_tensor_tensor(S_f[:, h], S_f[:, h],
                                           ebc_t[:, pbuf, h, cs:cs + 1],
                                           Stmp, Op.mult, Op.add),
                    deps=[s_dep[h], stm], inc="V")
                upd[hh // 4] = sf_dep[h]
                s_dep[h] = ctx.op("gpsimd", lambda e, h=h:
                                  e.tensor_copy(S_b[:, h], S_f[:, h]),
                                  deps=sf_dep[h], inc="P")
            gR = upd

        if SCAN_STOP == 'SADD':
            continue
        if cs == 1:
            ctx.dma("sync", S["o_s"][sp * 128:(sp + 1) * 128, :],
                    obuf, deps=oev, sem="St")


def emit_epi(ctx, ns):
    W, S, out = ns["W"], ns["S"], ns["out"]
    fb, bb, PF = ns["fb"], ns["bb"], ns["PF"]
    ones1_sb, rmsbc_sb = ns["ones1_sb"], ns["rmsbc_sb"]

    wo_sb = bb.take(16, 2048)
    otl = bb.take(1, 4, 2048)
    gtl = bb.take(1, 4, 2048)
    ogt = bb.take(4, 2048)
    ogf = bb.take(16, 512)
    xtl = fb.take(4, 2048)
    sums = fb.take(4, 16)
    rmsr = fb.take(4, 16)
    tmpf = fb.take(256)
    rrow = fb.take(128)

    wld = ctx.dma("scalar", wo_sb, S["Wo_b"].rearrange("(c p) n -> p c n", p=128), sem="WQ")
    rl = ctx.dma("sync", rrow[0:1, :], W["rmsw"].rearrange("(a d) -> a d", a=1))
    bm = ctx.op("tensor", lambda e:
                e.matmul(PF[0][:, 0:128], ones1_sb[:, :], rrow[0:1, :],
                         start=True, stop=True), deps=[rl], inc="T")
    rb = ctx.op("vector", lambda e:
                e.tensor_copy(rmsbc_sb[:, :], PF[0][:, 0:128]), deps=bm, inc="V")

    guards = {i: None for i in range(6)}
    guards[0] = rb
    oload = [None] * (NOTL + 2)
    gload = [None] * (NOTL + 2)
    tg = HALO
    oload[0] = ctx.dma("sync", otl[:, 0],
                       S["o_s"][tg:tg + 512, :]
                       .rearrange("(g p) d -> p g d", p=128), sem="OA")
    gload[0] = ctx.dma("sync", gtl[:, 0],
                       S["gate_s"][tg:tg + 512, :]
                       .rearrange("(g p) d -> p g d", p=128), sem="OA")

    prev_outst = None
    prev_ogst = None
    sq_guard = None
    for ti in range(NOTL):
        p = ti % 2
        tg = HALO + ti * 512
        xl = ctx.dma("sync", xtl, ns["x_ext"][3 + tg:3 + tg + 512, :]
                     .rearrange("(g p) d -> p g d", p=128),
                     deps=[("V", ctx.cnt["V"]), ctx.all_of("St")] if ti else None,
                     sem="XL")
        ogm = None
        for g in range(4):
            for h in range(NH):
                sqm = ctx.op("scalar", lambda e, p=p, g=g, h=h:
                             e.activation(tmpf[:, 128:256],
                                          otl[:, 0, g, h * 128:(h + 1) * 128],
                                          AF.Square,
                                          accum_out=sums[:, g, h:h + 1]),
                             deps=[gload[ti], sq_guard]
                             if (g == 0 and h == 0) else sq_guard, inc="A")
                sq_guard = sqm
            sa = ctx.op("scalar", lambda e, g=g:
                        e.activation(rmsr[:, g], sums[:, g], AF.Sqrt,
                                     scale=1.0 / 128, bias=1e-6),
                        deps=sq_guard, inc="A")
            rv = ctx.op("vector", lambda e, g=g:
                        e.reciprocal(rmsr[:, g], rmsr[:, g]), deps=sa, inc="V")
            for h in range(NH):
                dd = None
                if h == 0:
                    dd = [rv, gload[ti]]
                    if g == 0:
                        dd.append(ctx.all_of("St"))
                stm1 = ctx.op("vector", lambda e, p=p, g=g, h=h:
                              e.scalar_tensor_tensor(
                                  tmpf[:, 0:128],
                                  otl[:, 0, g, h * 128:(h + 1) * 128],
                                  rmsr[:, g, h:h + 1],
                                  gtl[:, 0, g, h * 128:(h + 1) * 128],
                                  Op.mult, Op.mult),
                              deps=(dd or []) + [ogm], inc="V")
                ogm = ctx.op("vector", lambda e, g=g, h=h:
                             e.tensor_tensor(ogt[:, g, h * 128:(h + 1) * 128],
                                             tmpf[:, 0:128], rmsbc_sb[:, :],
                                             Op.mult),
                             deps=stm1, inc="V")
        ogst = ctx.dma("scalar", S["og_s"][ti * 512:(ti + 1) * 512, :]
                       .rearrange("(g p) d -> p g d", p=128), ogt,
                       deps=ogm, sem="St")
        prev_ogst = ogst
        tl = None
        for cc in range(KC):
            tl = ctx.dma("sync", ogf[:, cc],
                         S["og_s"][ti * 512:(ti + 1) * 512, cc * 128:(cc + 1) * 128],
                         transpose=True, sem="TL",
                         deps=[ogst, ("T", ctx.cnt["T"])] if cc == 0 else None)
        stm = None
        for g in range(4):
            for hc in range(4):
                bk = 1 + (g * 4 + hc) % 2
                mm = None
                for cc in range(KC):
                    mm = ctx.op("tensor",
                                lambda e, g=g, hc=hc, cc=cc, bk=bk:
                                e.matmul(PF[bk][:, :],
                                         ogf[:, cc, g * 128:(g + 1) * 128],
                                         wo_sb[:, cc, hc * 512:(hc + 1) * 512],
                                         start=(cc == 0), stop=(cc == 15)),
                                deps=[tl, xl, ctx.all_of("WQ"), guards[bk]]
                                if cc == 0 else None,
                                inc="T" if cc == 15 else None)
                guards[bk] = ctx.op(
                    "vector", lambda e, g=g, hc=hc, bk=bk:
                    e.tensor_tensor(xtl[:, g, hc * 512:(hc + 1) * 512],
                                    xtl[:, g, hc * 512:(hc + 1) * 512],
                                    PF[bk][:, :], Op.add),
                    deps=mm, inc="V")
            stm = guards[1 + (g * 4 + 3) % 2]
        prev_outst = ctx.dma("scalar", out[ti * 512:(ti + 1) * 512, :]
                             .rearrange("(g p) d -> p g d", p=128), xtl,
                             deps=[guards[1], guards[2]], sem="St")
        if ti + 1 < NOTL:
            tg2 = HALO + (ti + 1) * 512
            oload[ti + 1] = ctx.dma("sync", otl[:, 0],
                                    S["o_s"][tg2:tg2 + 512, :]
                                    .rearrange("(g p) d -> p g d", p=128),
                                    deps=[("V", ctx.cnt["V"]),
                                          ("A", ctx.cnt["A"])], sem="OA")
            gload[ti + 1] = ctx.dma("sync", gtl[:, 0],
                                    S["gate_s"][tg2:tg2 + 512, :]
                                    .rearrange("(g p) d -> p g d", p=128),
                                    sem="OA")


def emit_ffn(ctx, ns):
    W, S, out = ns["W"], ns["S"], ns["out"]
    fb, bb, PF, PBF = ns["fb"], ns["bb"], ns["PF"], ns["PBF"]
    comb_sb, sg2_sb, ident_sb = ns["comb_sb"], ns["sg2_sb"], ns["ident_sb"]

    xnfm = bb.take(2, 16, 512)
    gupT = bb.take(16, 1536)
    dwnT = bb.take(6, 2048)
    actb = bb.take(6, 512)
    rwb = bb.take(2048)
    rwT = bb.take(16, 8)
    sgb = bb.take(2048)
    sgT = bb.take(16, 1)
    curf = fb.take(4, 2048)
    silf = fb.take(512)
    lgf = fb.take(4, 8)
    m1f = fb.take(4, 1)
    m2f = fb.take(4, 1)
    zf = fb.take(4, 8)
    gef = fb.take(4, 8)

    rld = ctx.dma("scalar", rwb[0:8, :], S["rw_b"], sem="WQ")
    sld = ctx.dma("scalar", sgb[0:1, :], S["sgw_b"], sem="WQ")
    tvm = None
    for kc in range(KC):
        tvm = ctx.op("tensor", lambda e, kc=kc:
                     e.transpose(PBF[0][:, kc * 8:(kc + 1) * 8],
                                 rwb[0:8, kc * 128:(kc + 1) * 128],
                                 ident_sb[0:8, 0:8]),
                     deps=ctx.all_of("WQ") if kc == 0 else None,
                     inc="T" if kc == 15 else None)
    rv1 = ctx.op("vector", lambda e:
                 e.tensor_copy(rwT.rearrange("p a b -> p (a b)"),
                               PBF[0][:, 0:128]), deps=tvm, inc="V")
    tvm2 = None
    for kc in range(KC):
        tvm2 = ctx.op("tensor", lambda e, kc=kc:
                      e.transpose(PBF[1][:, kc * 2:kc * 2 + 1],
                                  sgb[0:1, kc * 128:(kc + 1) * 128],
                                  ident_sb[0:1, 0:1]),
                      deps=[ctx.all_of("WQ"), rv1] if kc == 0 else None,
                      inc="T" if kc == 15 else None)
    rv2 = ctx.op("vector", lambda e:
                 e.tensor_copy(sgT,
                               PBF[1][:, 0:32].rearrange("p (a b) -> p a b", b=2)[:, :, 0:1]),
                 deps=tvm2, inc="V")

    load = xn_loader(ctx, ns, xnfm)
    xw = [load(1 + 0), load(1 + 1, deps=[("T", ctx.cnt["T"])])]
    g0, g1 = None, None
    for ti in range(NOTL):
        p = (1 + ti) % 2
        lm = None
        sm = None
        for g in range(4):
            for kc in range(KC):
                lm = ctx.op("tensor", lambda e, g=g, kc=kc, p=p:
                            e.matmul(PF[0][:, g * 8:(g + 1) * 8],
                                     xnfm[:, p, kc, g * 128:(g + 1) * 128],
                                     rwT[:, kc],
                                     start=(kc == 0), stop=(kc == 15)),
                            deps=[xw[ti], rv1, g0]
                            if (g == 0 and kc == 0) else None,
                            inc="T" if kc == 15 else None)
            for kc in range(KC):
                sm = ctx.op("tensor", lambda e, g=g, kc=kc, p=p:
                            e.matmul(PF[1][:, g:g + 1],
                                     xnfm[:, p, kc, g * 128:(g + 1) * 128],
                                     sgT[:, kc],
                                     start=(kc == 0), stop=(kc == 15)),
                            deps=[rv2, g1] if (g == 0 and kc == 0) else None,
                            inc="T" if (kc == 15 and g == 3) else None)
        gl = None
        for g in range(4):
            gl = ctx.op("vector", lambda e, g=g:
                        e.tensor_copy(lgf[:, g], PF[0][:, g * 8:(g + 1) * 8]),
                        deps=[lm, gl] if g == 0 else gl, inc="V")
            gl = ctx.op("vector", lambda e, g=g:
                        e.tensor_reduce(m1f[:, g], lgf[:, g], AX.X, Op.max),
                        deps=gl, inc="V")
            gl = ctx.op("vector", lambda e, g=g:
                        e.tensor_scalar(m1f[:, g], m1f[:, g], -1.0, None,
                                        Op.mult, Op.bypass), deps=gl, inc="V")
            az = ctx.op("scalar", lambda e, g=g:
                        e.activation(zf[:, g], lgf[:, g], AF.Exp,
                                     bias=m1f[:, g]), deps=gl, inc="A")
            gl = ctx.op("vector", lambda e, g=g:
                        e.tensor_single_scalar(gef[:, g], zf[:, g], 1.0,
                                               Op.is_ge), deps=az, inc="V")
            gl = ctx.op("vector", lambda e, g=g:
                        e.scalar_tensor_tensor(gef[:, g], gef[:, g], -2.0,
                                               zf[:, g], Op.mult, Op.add),
                        deps=gl, inc="V")
            gl = ctx.op("vector", lambda e, g=g:
                        e.tensor_reduce(m2f[:, g], gef[:, g], AX.X, Op.max),
                        deps=gl, inc="V")
            gl = ctx.op("vector", lambda e, g=g:
                        e.tensor_scalar(gef[:, g], zf[:, g], m2f[:, g], None,
                                        Op.is_ge, Op.bypass), deps=gl, inc="V")
            gl = ctx.op("vector", lambda e, g=g:
                        e.tensor_single_scalar(m2f[:, g], m2f[:, g], 1.0,
                                               Op.add), deps=gl, inc="V")
            gl = ctx.op("vector", lambda e, g=g:
                        e.reciprocal(m2f[:, g], m2f[:, g]), deps=gl, inc="V")
            gl = ctx.op("vector", lambda e, g=g, ti=ti:
                        e.tensor_tensor(gef[:, g], gef[:, g], zf[:, g],
                                        Op.mult), deps=gl, inc="V")
            gl = ctx.op("vector", lambda e, g=g, ti=ti:
                        e.tensor_scalar(comb_sb[:, ti, g], gef[:, g],
                                        m2f[:, g], None, Op.mult, Op.bypass),
                        deps=gl, inc="V")
        g0 = gl
        sgl = ctx.op("scalar", lambda e, ti=ti:
                     e.activation(sg2_sb[:, ti], PF[1][:, 0:4], AF.Sigmoid),
                     deps=sm, inc="A")
        g1 = sgl
        if ti + 2 < NOTL:
            xw.append(load(1 + ti + 2, deps=[("T", ctx.cnt["T"])]))
        else:
            xw.append(None)

    wait_cast = ("Lg", ctx.cnt["Lg"])

    def wload(le, half):
        zmarks = []
        if le < E:
            gsrc, usrc, dsrc = S["gup_b"][le], S["gup_b"][le], S["dwn_b"][le]
            gbase, ubase, ibase = 0, IMOE, 0
        else:
            sl = le - E
            gsrc, usrc, dsrc = S["shg_b"], S["shu_b"], S["shd_b"]
            gbase = ubase = sl * IMOE
            ibase = sl * IMOE
        i0 = half * 768
        ni = 768 if half == 0 else 640
        nic = ni // 128
        first = True
        for rz in range(3):
            rr = rz * 256
            w = min(256, ni - rr)
            zm = None
            for kc in range(KC):
                if FFN_PLAIN_W:
                    ctx.dma("scalar", gupT[:, kc, rr:rr + w],
                            gsrc[0:128, 0:w], sem=f"W{rz}",
                            deps=[wait_cast, ("T", ctx.cnt["T"])]
                            if first else None)
                    first = False
                    zm = ctx.dma("scalar", gupT[:, kc, 768 + rr:768 + rr + w],
                                 usrc[0:128, 0:w], sem=f"W{rz}")
                    continue
                ctx.dma("scalar", gupT[:, kc, rr:rr + w],
                        gsrc[gbase + i0 + rr:gbase + i0 + rr + w,
                             kc * 128:(kc + 1) * 128], transpose=True,
                        sem=f"W{rz}",
                        deps=[wait_cast, ("T", ctx.cnt["T"])] if first else None)
                first = False
                zm = ctx.dma("scalar", gupT[:, kc, 768 + rr:768 + rr + w],
                             usrc[ubase + i0 + rr:ubase + i0 + rr + w,
                                  kc * 128:(kc + 1) * 128], transpose=True,
                             sem=f"W{rz}")
            zmarks.append(zm)
        dm = None
        for ic in range(nic):
            if le < E:
                src = dsrc[:, i0 + ic * 128:i0 + (ic + 1) * 128]
            else:
                src = dsrc[:, ibase + i0 + ic * 128:ibase + i0 + (ic + 1) * 128]
            if FFN_PLAIN_W:
                dm = ctx.dma("scalar", dwnT[:, ic], gsrc[0:128, 0:2048],
                             sem="WL")
            else:
                dm = ctx.dma("scalar", dwnT[:, ic], src, transpose=True,
                             sem="WL")
        return zmarks, dm, nic

    prev_acc = [None] * NOTL
    gu_guard = [g0, g1]
    dn_guard = [None, None]
    ffn_silf_guard = None
    for le in range(E + 4):
        for half in range(2):
            wmz, wmd, nic = wload(le, half)
            xw2 = [load(1 + 0, deps=[("T", ctx.cnt["T"])]),
                   load(1 + 1, deps=[("T", ctx.cnt["T"])])]
            for ti in range(NOTL):
                p = (1 + ti) % 2
                act_marks = []
                for j in range(nic):
                    gm = None
                    for kc in range(KC):
                        gm = ctx.op("tensor", lambda e, j=j, kc=kc, p=p:
                                    e.matmul(PF[0][:, :],
                                             gupT[:, kc, j * 128:(j + 1) * 128],
                                             xnfm[:, p, kc],
                                             start=(kc == 0), stop=(kc == 15)),
                                    deps=[wmz[j // 2], xw2[ti], gu_guard[0]]
                                    if kc == 0 else None,
                                    inc="T" if kc == 15 else None)
                    um = None
                    for kc in range(KC):
                        um = ctx.op("tensor", lambda e, j=j, kc=kc, p=p:
                                    e.matmul(PF[1][:, :],
                                             gupT[:, kc,
                                                  768 + j * 128:768 + (j + 1) * 128],
                                             xnfm[:, p, kc],
                                             start=(kc == 0), stop=(kc == 15)),
                                    deps=[wmz[j // 2], gu_guard[1]]
                                    if kc == 0 else None,
                                    inc="T" if kc == 15 else None)
                    sa = ctx.op("scalar", lambda e:
                                e.activation(silf, PF[0][:, :], AF.Silu),
                                deps=[gm, ffn_silf_guard], inc="A")
                    gu_guard[0] = sa
                    av = ctx.op("vector", lambda e, j=j:
                                e.tensor_tensor(actb[:, j], silf, PF[1][:, :],
                                                Op.mult), deps=[sa, um], inc="V")
                    gu_guard[1] = av
                    ffn_silf_guard = av
                    act_marks.append(av)
                ev = None
                for g in range(4):
                    for hc in range(4):
                        bk = 2 + (g * 4 + hc) % 2
                        mm = None
                        for ic in range(nic):
                            mm = ctx.op(
                                "tensor",
                                lambda e, g=g, hc=hc, ic=ic, bk=bk, nic=nic:
                                e.matmul(PF[bk][:, :],
                                         actb[:, ic, g * 128:(g + 1) * 128],
                                         dwnT[:, ic, hc * 512:(hc + 1) * 512],
                                         start=(ic == 0), stop=(ic == nic - 1)),
                                deps=(act_marks + [wmd, dn_guard[bk - 2]])
                                if ic == 0 else None,
                                inc="T" if ic == nic - 1 else None)
                        if le < E:
                            scl = comb_sb[:, ti, g, le:le + 1]
                        else:
                            scl = sg2_sb[:, ti, g:g + 1]
                        ev = ctx.op(
                            "scalar",
                            lambda e, g=g, hc=hc, bk=bk, scl=scl:
                            e.activation(curf[:, g, hc * 512:(hc + 1) * 512],
                                         PF[bk][:, :], AF.Copy, scale=scl),
                            deps=[mm, ctx.all_of("G")], inc="A")
                        dn_guard[bk - 2] = ev
                prev_acc[ti] = ctx.dma(
                    "gpsimd", out[ti * 512:(ti + 1) * 512, :]
                    .rearrange("(g p) d -> p g d", p=128), curf,
                    deps=ev, sem="G", accum=not FFN_NO_ACCUM)
                if ti + 2 < NOTL:
                    xw2.append(load(1 + ti + 2, deps=[("T", ctx.cnt["T"])]))
                else:
                    xw2.append(None)


def _host_consts():
    tril = np.tril(np.ones((CHK, CHK), np.float32), -1)
    triu = np.triu(np.ones((CHK, CHK), np.float32))
    mN = np.zeros((128, 512), np.float32)
    mU = np.zeros((128, 512), np.float32)
    for j in range(4):
        mN[0:64, j * 128:j * 128 + 64] = -tril
        mN[64:128, j * 128 + 64:j * 128 + 128] = -tril
        mU[0:64, j * 128:j * 128 + 64] = triu
        mU[64:128, j * 128 + 64:j * 128 + 128] = triu
    ident = np.eye(128, dtype=ml_dtypes.bfloat16)
    return ident, mN, mU


NAMEMAP = {"Wq": "L0_attn_Wq", "Wk": "L0_attn_Wk", "Wv": "L0_attn_Wv",
           "Wo": "L0_attn_Wo", "Wad": "L0_attn_Wad", "Wau": "L0_attn_Wau",
           "Wb": "L0_attn_Wb", "Wgd": "L0_attn_Wgd", "Wgu": "L0_attn_Wgu",
           "cqw": "L0_attn_conv_q_w", "ckw": "L0_attn_conv_k_w",
           "cvw": "L0_attn_conv_v_w", "rmsw": "L0_attn_rms_w",
           "rw": "L0_ffn_router_weight", "gup": "L0_ffn_gate_up_proj",
           "dwn": "L0_ffn_down_proj", "shg": "L0_ffn_sh_gate_w",
           "shu": "L0_ffn_sh_up_w", "shd": "L0_ffn_sh_down_w",
           "sgw": "L0_ffn_shared_gate_w"}


def make_in_maps(inputs):
    x = np.ascontiguousarray(inputs["L0_attn_x"])
    ident, mN, mU = _host_consts()
    weights = {k: np.ascontiguousarray(inputs[v]) for k, v in NAMEMAP.items()}
    in_maps = []
    for c in range(8):
        b, q = c // 4, c % 4
        t0 = q * TOWN
        ext0 = t0 - HALO - 3
        if ext0 < 0:
            xe = np.concatenate([np.zeros((-ext0, D), np.float32),
                                 x[b, :t0 + TOWN]], axis=0)
        else:
            xe = x[b, ext0:t0 + TOWN]
        m = {"x_ext": np.ascontiguousarray(xe), "identb": ident,
             "maskN": mN, "maskU": mU}
        m.update(weights)
        in_maps.append(m)
    return in_maps


def kernel(**inputs):
    in_maps = make_in_maps(inputs)
    nc, stack = build_program()
    try:
        res = run_bass_kernel_spmd(nc, in_maps, core_ids=list(range(8)))
    finally:
        stack.close()
    outs = [res.results[c]["out"] for c in range(8)]
    return np.stack([np.concatenate(outs[0:4], 0),
                     np.concatenate(outs[4:8], 0)], 0)
